# version 03
# baseline (speedup 1.0000x reference)
import numpy as np
import ml_dtypes
import contextlib
import concourse.bass as bass
import concourse.mybir as mybir
from concourse.bass_utils import run_bass_kernel_spmd

F32 = mybir.dt.float32
BF16 = mybir.dt.bfloat16
AF = mybir.ActivationFunctionType
ALU = mybir.AluOpType
AX = mybir.AxisListType

NCORE = 8
TOK = 16384
TPC = TOK // NCORE
D = 1024
BF = ml_dtypes.bfloat16


class Prog:
    ENGINES = ("sync", "tensor", "vector", "scalar", "gpsimd")
    SERIAL = False

    def __init__(self):
        self.stages = []
        self.cur = []

    def op(self, eng, fn, dma=False):
        assert eng in self.ENGINES
        self.cur.append((eng, fn, 16 if dma else 1))
        if Prog.SERIAL:
            self.bar()

    def bar(self):
        if self.cur:
            self.stages.append(self.cur)
            self.cur = []

    def emit(self, block, sem):
        self.bar()
        per = {e: [] for e in self.ENGINES}
        cum = 0
        for stage in self.stages:
            start = cum
            for eng, fn, inc in stage:
                per[eng].append((fn, start, inc))
                cum += inc
        total = cum

        def mk(items, is_sync):
            def body(eng):
                last_wait = 0
                for fn, wait, inc in items:
                    if wait > last_wait:
                        eng.wait_ge(sem, wait)
                        last_wait = wait
                    fn(eng).then_inc(sem, inc)
                if is_sync:
                    eng.wait_ge(sem, total)
            return body

        for e in self.ENGINES:
            if per[e] or e == "sync":
                getattr(block, e)(mk(per[e], e == "sync"))


class Ctx:
    def __init__(self):
        self.nc = bass.Bass("TRN2", target_bir_lowering=False)
        self.st = contextlib.ExitStack()
        self.p = Prog()
        self._n = 0

    def sb(self, shape, dt, name=None):
        self._n += 1
        return self.st.enter_context(
            self.nc.sbuf_tensor(name or f"sb{self._n}", shape, dt))

    def ps(self, shape, dt=F32, name=None):
        self._n += 1
        return self.st.enter_context(
            self.nc.psum_tensor(name or f"ps{self._n}", shape, dt))

    def inp(self, name, shape, dt=F32):
        return self.nc.dram_tensor(name, shape, dt, kind="ExternalInput").ap()

    def out(self, name, shape, dt=F32):
        return self.nc.dram_tensor(name, shape, dt, kind="ExternalOutput").ap()

    def setup_common(self, consts=()):
        self.ones_col = self.sb([128, 1], BF16, "ones_col")
        self.ones_row = self.sb([1, 128], F32, "ones_row")
        self.ones_row_b = self.sb([1, 128], BF16, "ones_row_b")
        p = self.p
        p.op("vector", lambda e: e.memset(self.ones_col[:], 1.0))
        p.op("vector", lambda e: e.memset(self.ones_row[:], 1.0))
        p.op("vector", lambda e: e.memset(self.ones_row_b[:], 1.0))
        cl = sorted(set(float(x) for x in consts))
        if cl:
            self.cpool = self.sb([128, len(cl)], F32, "cpool")
            for i, cv in enumerate(cl):
                p.op("vector", lambda e, i=i, cv=cv: e.memset(self.cpool[:, i:i + 1], cv))
                self.nc.const_aps.register(F32, cv, self.cpool[:, i:i + 1])
        p.bar()

    def run(self, in_maps, core_ids, **kw):
        nc = self.finalize()
        return run_bass_kernel_spmd(nc, in_maps, core_ids=core_ids, **kw)

    def finalize(self):
        block = self.st.enter_context(self.nc.Block())
        sem = self.st.enter_context(self.nc.semaphore("SEQ"))
        with self.st:
            self.p.emit(block, sem)
        return self.nc


def ln_stats(c, xb, C, T, eps, ps2, rows=None):
    p = c.p
    nch = C // 128
    if rows is None:
        sqb = c.sb([128, 512], BF16)
        mu = c.sb([1, T], F32)
        m2 = c.sb([1, T], F32)
        rstd = c.sb([1, T], F32)
        mur = c.sb([1, T], F32)
    else:
        sqb, mu, m2, rstd, mur = rows["sqb"], rows["mu"], rows["m2"], rows["rstd"], rows["mur"]
    for sl in range(0, T, 512):
        for ch in range(nch):
            xap = xb[:, ch * T + sl: ch * T + sl + 512]
            p.op("vector", lambda e, xap=xap: e.tensor_mul(sqb[:], xap, xap))
            p.bar()
            p.op("tensor", lambda e, xap=xap, ch=ch: e.matmul(
                ps2[0:1, 0:512], c.ones_col[:], xap, start=(ch == 0), stop=(ch == nch - 1)))
            p.op("tensor", lambda e, ch=ch: e.matmul(
                ps2[0:1, 512:1024], c.ones_col[:], sqb[:], start=(ch == 0), stop=(ch == nch - 1)))
            p.bar()
        p.op("vector", lambda e, sl=sl: e.tensor_scalar(
            mu[0:1, sl:sl + 512], ps2[0:1, 0:512], 1.0 / C, None, ALU.mult, ALU.bypass))
        p.op("scalar", lambda e, sl=sl: e.mul(m2[0:1, sl:sl + 512], ps2[0:1, 512:1024], 1.0 / C))
        p.bar()
    p.op("vector", lambda e: e.tensor_mul(rstd[:], mu[:], mu[:]))
    p.bar()
    p.op("vector", lambda e: e.tensor_sub(rstd[:], m2[:], rstd[:]))
    p.bar()
    p.op("scalar", lambda e: e.activation(m2[:], rstd[:], AF.Sqrt, bias=float(eps), scale=1.0))
    p.bar()
    p.op("vector", lambda e: e.reciprocal(rstd[:], m2[:]))
    p.bar()
    p.op("vector", lambda e: e.tensor_mul(mur[:], mu[:], rstd[:]))
    p.bar()
    return {"mu": mu, "rstd": rstd, "mur": mur, "m2": m2, "sqb": sqb}


def bcast_row(c, row, T, ps1, out=None, dt=F32):
    p = c.p
    out = out if out is not None else c.sb([128, T], dt)
    for sl in range(0, T, 512):
        p.op("tensor", lambda e, sl=sl: e.matmul(
            ps1[:, 0:512], c.ones_row[:], row[0:1, sl:sl + 512], start=True, stop=True))
        p.bar()
        p.op("vector", lambda e, sl=sl: e.tensor_copy(out[:, sl:sl + 512], ps1[:, 0:512]))
        p.bar()
    return out


def load_w_bf16(c, hbm, C, F, name=None):
    p = c.p
    t = c.sb([128, (C // 128) * F], BF16, name)
    for ch in range(C // 128):
        p.op("gpsimd", lambda e, ch=ch: e.dma_start(
            out=t[:, ch * F:(ch + 1) * F], in_=hbm[ch * 128:(ch + 1) * 128, :]), dma=True)
    return t


def colsum_neg_row(c, wt, C, F, ps2):
    p = c.p
    nch = C // 128
    out = c.sb([1, F], F32)
    for sl in range(0, F, 512):
        w = min(512, F - sl)
        for ch in range(nch):
            p.op("tensor", lambda e, ch=ch, sl=sl, w=w: e.matmul(
                ps2[0:1, 0:w], c.ones_col[:], wt[:, ch * F + sl:ch * F + sl + w],
                start=(ch == 0), stop=(ch == nch - 1)))
        p.bar()
        p.op("vector", lambda e, sl=sl, w=w: e.tensor_scalar(
            out[0:1, sl:sl + w], ps2[0:1, 0:w], -1.0, None, ALU.mult, ALU.bypass))
        p.bar()
    return out


def ln_proj(c, xb, C, T, wt, Fdim, mu_row, rstd_bc, psA, out_cb, bias_col=None,
            act=None, out_dt=BF16, negcs_f32=None, extra_rank1=None, wt_stride=None):
    p = c.p
    nch = C // 128
    nfo = Fdim // 128
    WS = wt_stride if wt_stride is not None else Fdim
    for fo in range(nfo):
        for sl in range(0, T, 512):
            for ch in range(nch):
                p.op("tensor", lambda e, fo=fo, sl=sl, ch=ch: e.matmul(
                    psA[:, 0:512], wt[:, ch * WS + fo * 128: ch * WS + fo * 128 + 128],
                    xb[:, ch * T + sl: ch * T + sl + 512],
                    start=(ch == 0), stop=False))
            p.op("tensor", lambda e, fo=fo, sl=sl: e.matmul(
                psA[:, 0:512], negcs_f32[0:1, fo * 128:fo * 128 + 128],
                mu_row[0:1, sl:sl + 512], start=False, stop=True))
            p.bar()
            tgt = out_cb(fo, sl)
            if act is None and bias_col is None:
                p.op("vector", lambda e, fo=fo, sl=sl, tgt=tgt: e.tensor_mul(
                    tgt, psA[:, 0:512], rstd_bc[:, sl:sl + 512]))
                p.bar()
            else:
                tmp = c.sb([128, 512], F32, None) if not hasattr(c, "_lnp_tmp") else c._lnp_tmp
                c._lnp_tmp = tmp
                p.op("vector", lambda e, sl=sl, tmp=tmp: e.tensor_mul(
                    tmp[:], psA[:, 0:512], rstd_bc[:, sl:sl + 512]))
                p.bar()
                bias = bias_col[:, fo:fo + 1] if bias_col is not None else 0.0
                if act is None or act == AF.Copy:
                    p.op("vector", lambda e, tmp=tmp, bias=bias, tgt=tgt: e.tensor_scalar(
                        tgt, tmp[:], bias, None, ALU.add, ALU.bypass))
                else:
                    p.op("scalar", lambda e, tmp=tmp, bias=bias, fn=act, tgt=tgt: e.activation(
                        tgt, tmp[:], fn, bias=bias, scale=1.0))
                p.bar()


def shard_xT(x2d):
    return [np.ascontiguousarray(x2d[i * TPC:(i + 1) * TPC].T) for i in range(NCORE)]

def build_L1():
    c = Ctx()
    T = TPC
    x_h = c.inp("xT", [D, T], F32)
    dw_h = c.inp("down_w", [D, 256], F32)
    db_h = c.inp("down_b", [128, 2], F32)
    h_o = c.out("hT", [256, T], BF16)
    c.setup_common(consts=[0.0, 1.0, 1e-5])
    p = c.p
    xb = c.sb([128, 8 * T], BF16, "xb")
    for ch in range(8):
        p.op("gpsimd", lambda e, ch=ch: e.dma_start(
            out=xb[:, ch * T:(ch + 1) * T], in_=x_h[ch * 128:(ch + 1) * 128, :]), dma=True)
    wt = load_w_bf16(c, dw_h, D, 256, "dw")
    dbc = c.sb([128, 2], F32, "dbc")
    p.op("sync", lambda e: e.dma_start(out=dbc[:], in_=db_h[:]), dma=True)
    p.bar()
    ps2 = c.ps([1, 1024], name="ps_stat")
    psA = c.ps([128, 512], name="psA")
    stats = ln_stats(c, xb, D, T, 1e-5, ps2)
    negcs = colsum_neg_row(c, wt, D, 256, ps2)
    rstd_bc = bcast_row(c, stats["rstd"], T, psA)
    hT = c.sb([128, 2 * T], BF16, "hT_sb")

    def out_cb(fo, sl):
        return hT[:, fo * T + sl: fo * T + sl + 512]

    ln_proj(c, xb, D, T, wt, 256, stats["mu"], rstd_bc, psA, out_cb,
            bias_col=dbc, act=AF.Silu, negcs_f32=negcs)
    for fo in range(2):
        p.op("sync", lambda e, fo=fo: e.dma_start(
            out=h_o[fo * 128:(fo + 1) * 128, :], in_=hT[:, fo * T:(fo + 1) * T]), dma=True)
    p.bar()
    return c


def host_L1(inputs):
    x = np.asarray(inputs["L0_attn_x"], np.float32).reshape(TOK, D)
    xTs = shard_xT(x)
    dw = np.ascontiguousarray(np.asarray(inputs["L0_attn_down_w"], np.float32))
    db = np.ascontiguousarray(np.asarray(inputs["L0_attn_down_b"], np.float32).reshape(2, 128).T)
    return [{"xT": xTs[i], "down_w": dw, "down_b": db} for i in range(NCORE)]

def build_L2(NM=512, dbg=False):
    K = 16
    L = NM * K
    c = Ctx()
    p = c.p
    h_h = c.inp("hT", [256, L], BF16)
    wq_h = c.inp("wq", [256, 128], F32)
    wk_h = c.inp("wk", [256, 128], F32)
    wv_h = c.inp("wv", [256, 128], F32)
    lrw_h = c.inp("lrw", [128, 2], F32)
    lrb_h = c.inp("lrb", [1, 1], F32)
    tokc_h = c.inp("tokc", [16, 1], F32)
    invc_h = c.inp("invc", [16, 1], F32)
    maskT_h = c.inp("maskT", [16, 16], F32)
    identf_h = c.inp("identf", [128, 128], F32)
    W1_h = c.inp("W1", [128, 128], F32)
    b1_h = c.inp("b1", [1, 128], F32)
    gw_h = c.inp("gw_row", [1, 128], F32)
    gb_h = c.inp("gb_row", [1, 128], F32)
    gwc_h = c.inp("gw_col", [128, 1], F32)
    outs_h = c.out("outs", [L, 128], BF16)
    c.setup_common(consts=[0.0, 1.0, 1e-6, 1.0 / 128, -1.0 / 128, 1.0 / K])
    hT = c.sb([128, 2 * L], BF16, "hT_sb")
    xqT = c.sb([128, L], BF16, "xqT")
    xkT = c.sb([128, L], BF16, "xkT")
    tgtT = c.sb([128, L], BF16, "tgtT")
    wqs = c.sb([128, 2 * 128], BF16, "wqs")
    wks = c.sb([128, 2 * 128], BF16, "wks")
    wvs = c.sb([128, 2 * 128], BF16, "wvs")
    lrww = c.sb([128, 2], BF16, "lrww")
    lrow = c.sb([1, L], F32, "lrow")
    identf = c.sb([128, 128], F32, "identf_sb")
    identb = c.sb([128, 128], BF16, "identb_sb")
    maskT = c.sb([16, 16], F32, "maskT_sb")
    masktokT = c.sb([16, 16], F32, "masktokT")
    tokc = c.sb([16, 1], F32, "tokc_sb")
    invc = c.sb([16, 1], F32, "invc_sb")
    t15c = c.sb([16, 1], F32, "t15c")
    gwro = c.sb([1, 128], F32, "gwro")
    gbro = c.sb([1, 128], F32, "gbro")
    gw2r = c.sb([1, 128], F32, "gw2r")
    gbgw = c.sb([1, 128], F32, "gbgw")
    gbgwb = c.sb([16, 128], F32, "gbgwb")
    sgbc = c.sb([16, 1], F32, "sgbc")
    sgb1 = c.sb([1, 1], F32, "sgb1")
    gw2b = c.sb([16, 128], F32, "gw2b")
    gwb16 = c.sb([16, 1024], F32, "gwb16")
    gbb16 = c.sb([16, 1024], F32, "gbb16")
    W1sb = c.sb([128, 2 * 128], F32, "W1sb")
    W1bf = c.sb([128, 2 * 128], BF16, "W1bf")
    b1r = c.sb([1, 2 * 128], F32, "b1r")
    b1bf = c.sb([1, 2 * 128], BF16, "b1bf")
    o1 = c.sb([16, 16], F32, "o1")
    nct = c.sb([16, 16], BF16, "nct")
    s_ = c.sb([16, 1], F32, "s_")
    ssq = c.sb([16, 1], F32, "ssq")
    zsq = c.sb([16, 128], F32, "zsq")
    mu = c.sb([16, 1], F32, "mu")
    m2 = c.sb([16, 1], F32, "m2")
    u_ = c.sb([16, 1], F32, "u_")
    stdv = c.sb([16, 1], F32, "stdv")
    rstd = c.sb([16, 1], F32, "rstd")
    xh = c.sb([16, 128], F32, "xh")
    t1 = c.sb([16, 128], F32, "t1")
    st1 = c.sb([16, 1], F32, "st1")
    sgg = c.sb([16, 1], F32, "sgg")
    gy = c.sb([16, 128], F32, "gy")
    u1 = c.sb([16, 128], F32, "u1")
    mgy = c.sb([16, 1], F32, "mgy")
    q5 = c.sb([16, 128], F32, "q5")
    sgx = c.sb([16, 1], F32, "sgx")
    nmgyx = c.sb([16, 1], F32, "nmgyx")
    aa = c.sb([16, 128], F32, "aa")
    grad = c.sb([16, 128], BF16, "grad")
    sct = c.sb([16, 1], F32, "sct")
    leta = c.sb([16, 1], F32, "leta")
    letab = c.sb([16, 1], BF16, "letab")
    lk = c.sb([16, 128], BF16, "lk")
    xqtok = c.sb([16, 1024], F32, "xqtok")
    zbb = c.sb([16, 1024], F32, "zbb")
    srow = c.sb([16, 8], F32, "srow")
    sqrow = c.sb([16, 8], F32, "sqrow")
    murow = c.sb([16, 8], F32, "murow")
    m2row = c.sb([16, 8], F32, "m2row")
    urow = c.sb([16, 8], F32, "urow")
    stdrow = c.sb([16, 8], F32, "stdrow")
    rstdrow = c.sb([16, 8], F32, "rstdrow")
    zb2 = c.sb([16, 1024], F32, "zb2")
    outc = c.sb([16, 1024], BF16, "outc")
    pbig = c.ps([128, 512], name="pbig")
    pw1 = c.ps([128, 128], name="pw1")
    pzA = c.ps([16, 128], name="pzA")
    pD = c.ps([16, 512], name="pD")
    pgg = c.ps([16, 128], BF16, name="pgg")
    pxq = c.ps([16, 128], BF16, name="pxq")
    pxk = c.ps([16, 128], BF16, name="pxk")
    praw = pD[0:16, 0:16]
    plr = pD[0:16, 16:17]
    pb1 = pD[0:1, 32:160]
    pzbar = pbig[0:16, 0:128]

    for ch in range(2):
        p.op("sync", lambda e, ch=ch: e.dma_start(out=hT[:, ch * L:(ch + 1) * L], in_=h_h[ch * 128:(ch + 1) * 128, :]), dma=True)
        for t, hh in ((wqs, wq_h), (wks, wk_h), (wvs, wv_h)):
            p.op("gpsimd", lambda e, t=t, ch=ch, hh=hh: e.dma_start(
                out=t[:, ch * 128:(ch + 1) * 128], in_=hh[ch * 128:(ch + 1) * 128, :]), dma=True)
    p.op("gpsimd", lambda e: e.dma_start(out=lrww[:], in_=lrw_h[:]), dma=True)
    p.op("sync", lambda e: e.dma_start(out=identf[:], in_=identf_h[:]), dma=True)
    p.op("gpsimd", lambda e: e.dma_start(out=identb[:], in_=identf_h[:]), dma=True)
    p.op("sync", lambda e: e.dma_start(out=maskT[:], in_=maskT_h[:]), dma=True)
    p.op("sync", lambda e: e.dma_start(out=tokc[:], in_=tokc_h[:]), dma=True)
    p.op("sync", lambda e: e.dma_start(out=invc[:], in_=invc_h[:]), dma=True)
    p.op("sync", lambda e: e.dma_start(out=W1sb[:, 0:128], in_=W1_h[:]), dma=True)
    p.op("sync", lambda e: e.dma_start(out=b1r[0:1, 0:128], in_=b1_h[:]), dma=True)
    p.op("sync", lambda e: e.dma_start(out=gwro[:], in_=gw_h[:]), dma=True)
    p.op("sync", lambda e: e.dma_start(out=gbro[:], in_=gb_h[:]), dma=True)
    lrbt = c.sb([1, 1], F32, "lrbt")
    p.op("sync", lambda e: e.dma_start(out=lrbt[:], in_=lrb_h[:]), dma=True)
    gwcol = c.sb([128, 1], F32, "gwcol")
    p.op("sync", lambda e: e.dma_start(out=gwcol[:], in_=gwc_h[:]), dma=True)
    p.bar()
    p.op("vector", lambda e: e.tensor_add(tokc[:], tokc[:], invc[:]))
    p.bar()
    p.op("vector", lambda e: e.tensor_scalar_max(tokc[:], tokc[:], 0.0))
    p.bar()
    tokrow = c.sb([1, 16], F32, "tokrow")
    p.op("tensor", lambda e: e.matmul(pD[0:1, 256:272], tokc[:], identf[0:16, 0:16], start=True, stop=True))
    p.bar()
    p.op("vector", lambda e: e.tensor_copy(tokrow[:], pD[0:1, 256:272]))
    p.bar()
    p.op("tensor", lambda e: e.matmul(praw, c.ones_row[0:1, 0:16], tokrow[:], start=True, stop=True))
    p.bar()
    p.op("vector", lambda e: e.tensor_mul(masktokT[:], maskT[:], praw))
    p.bar()
    p.op("tensor", lambda e: e.matmul(pD[0:1, 272:273], tokc[:], identf[0:16, 15:16], start=True, stop=True))
    p.bar()
    tok15s = c.sb([1, 1], F32, "tok15s")
    p.op("vector", lambda e: e.tensor_copy(tok15s[:], pD[0:1, 272:273]))
    p.bar()
    p.op("tensor", lambda e: e.matmul(pD[0:16, 288:289], c.ones_row[0:1, 0:16], tok15s[:], start=True, stop=True))
    p.bar()
    p.op("vector", lambda e: e.tensor_scalar(t15c[:], pD[0:16, 288:289], -1.0, None, ALU.mult, ALU.bypass))
    p.bar()
    p.op("vector", lambda e: e.tensor_mul(gw2r[:], gwro[:], gwro[:]))
    p.op("gpsimd", lambda e: e.tensor_mul(gbgw[:], gbro[:], gwro[:]))
    p.bar()
    p.op("tensor", lambda e: e.matmul(pbig[0:16, 0:128], c.ones_row[0:1, 0:16], gw2r[:], start=True, stop=True))
    p.bar()
    p.op("vector", lambda e: e.tensor_copy(gw2b[:], pbig[0:16, 0:128]))
    p.bar()
    p.op("tensor", lambda e: e.matmul(pbig[0:16, 0:128], c.ones_row[0:1, 0:16], gbgw[:], start=True, stop=True))
    p.bar()
    p.op("vector", lambda e: e.tensor_copy(gbgwb[:], pbig[0:16, 0:128]))
    p.bar()
    p.op("vector", lambda e: e.reduce_sum(sgb1[:], gbgw[:], axis=AX.X))
    p.bar()
    p.op("tensor", lambda e: e.matmul(pbig[0:16, 128:129], c.ones_row[0:1, 0:16], sgb1[:], start=True, stop=True))
    p.bar()
    p.op("vector", lambda e: e.tensor_copy(sgbc[:], pbig[0:16, 128:129]))
    p.bar()
    for r in range(8):
        p.op("tensor", lambda e: e.matmul(pbig[0:16, 0:128], c.ones_row[0:1, 0:16], gwro[:], start=True, stop=True))
        p.bar()
        p.op("vector", lambda e, r=r: e.tensor_copy(gwb16[:, r * 128:(r + 1) * 128], pbig[0:16, 0:128]))
        p.bar()
        p.op("tensor", lambda e: e.matmul(pbig[0:16, 0:128], c.ones_row[0:1, 0:16], gbro[:], start=True, stop=True))
        p.bar()
        p.op("vector", lambda e, r=r: e.tensor_copy(gbb16[:, r * 128:(r + 1) * 128], pbig[0:16, 0:128]))
        p.bar()
    p.op("vector", lambda e: e.tensor_copy(W1bf[:, 0:128], W1sb[:, 0:128]))
    p.op("scalar", lambda e: e.activation(b1bf[0:1, 0:128], b1r[0:1, 0:128], AF.Copy, bias=0.0, scale=1.0))
    p.bar()
    for (dst, wt) in ((xqT, wqs), (xkT, wks), (tgtT, wvs)):
        for sl in range(0, L, 512):
            for ch in range(2):
                p.op("tensor", lambda e, dst=dst, wt=wt, sl=sl, ch=ch: e.matmul(
                    pbig[:, 0:512], wt[:, ch * 128:(ch + 1) * 128],
                    hT[:, ch * L + sl: ch * L + sl + 512], start=(ch == 0), stop=(ch == 1)))
            p.bar()
            p.op("vector", lambda e, dst=dst, sl=sl: e.tensor_copy(dst[:, sl:sl + 512], pbig[:, 0:512]))
            p.bar()
    _w = min(2048, L)
    for sl in range(0, L, _w):
        p.op("vector", lambda e, sl=sl, _w=_w: e.tensor_sub(tgtT[:, sl:sl + _w], tgtT[:, sl:sl + _w], xkT[:, sl:sl + _w]))
        p.bar()
    for sl in range(0, L, 512):
        for ch in range(2):
            p.op("tensor", lambda e, sl=sl, ch=ch: e.matmul(
                pD[0:1, 0:512], lrww[:, ch:ch + 1], hT[:, ch * L + sl:ch * L + sl + 512],
                start=(ch == 0), stop=(ch == 1)))
        p.bar()
        p.op("scalar", lambda e, sl=sl: e.activation(lrow[0:1, sl:sl + 512], pD[0:1, 0:512], AF.Sigmoid, bias=lrbt[0:1, 0:1], scale=1.0))
        p.bar()
    for n in range(NM):
        par = n % 2
        q = n % 8
        t0 = n * K
        W1b_cur = W1bf[:, par * 128:(par + 1) * 128]
        W1s_cur = W1sb[:, par * 128:(par + 1) * 128]
        W1b_nxt = W1bf[:, (1 - par) * 128:(2 - par) * 128]
        W1s_nxt = W1sb[:, (1 - par) * 128:(2 - par) * 128]
        b1_cur = b1r[0:1, par * 128:(par + 1) * 128]
        b1b_cur = b1bf[0:1, par * 128:(par + 1) * 128]
        b1_nxt = b1r[0:1, (1 - par) * 128:(2 - par) * 128]
        b1b_nxt = b1bf[0:1, (1 - par) * 128:(2 - par) * 128]
        xk_sl = xkT[:, t0:t0 + K]
        xq_sl = xqT[:, t0:t0 + K]
        tg_sl = tgtT[:, t0:t0 + K]
        p.op("tensor", lambda e, xk_sl=xk_sl, xq_sl=xq_sl: e.matmul(praw, xk_sl, xq_sl, start=True, stop=True))
        p.op("tensor", lambda e, n=n: e.transpose(plr, lrow[0:1, n * K:(n + 1) * K], identf[0:1, 0:1]))
        p.op("tensor", lambda e, tg_sl=tg_sl: e.transpose(pgg[:], tg_sl, identb[:]))
        p.op("tensor", lambda e, xq_sl=xq_sl: e.transpose(pxq[:], xq_sl, identb[:]))
        p.op("tensor", lambda e, xk_sl=xk_sl: e.transpose(pxk[:], xk_sl, identb[:]))
        p.op("tensor", lambda e, xk_sl=xk_sl, W1b_cur=W1b_cur: e.matmul(pzA[:], xk_sl, W1b_cur, start=True, stop=False))
        p.op("tensor", lambda e, b1b_cur=b1b_cur: e.matmul(pzA[:], c.ones_row_b[0:1, 0:16], b1b_cur, start=False, stop=True))
        p.op("tensor", lambda e, xq_sl=xq_sl, W1b_cur=W1b_cur: e.matmul(pzbar, xq_sl, W1b_cur, start=True, stop=False))
        p.op("tensor", lambda e, b1b_cur=b1b_cur: e.matmul(pzbar, c.ones_row_b[0:1, 0:16], b1b_cur, start=False, stop=False, skip_group_check=True))
        p.bar()
        p.op("vector", lambda e: e.reduce_sum(s_[:], pzA[:], axis=AX.X))
        p.op("scalar", lambda e, q=q: e.activation(xqtok[:, q * 128:(q + 1) * 128], pxq[:], AF.Copy, bias=0.0, scale=1.0))
        p.bar()
        p.op("vector", lambda e: e.scalar_tensor_tensor(o1[:], praw, 1.0, masktokT[:], ALU.add, ALU.mult))
        p.op("scalar", lambda e: e.activation(zsq[:], pzA[:], AF.Square, bias=0.0, scale=1.0, accum_out=ssq[:]))
        p.bar()
        p.op("vector", lambda e: e.tensor_scalar(mu[:], s_[:], 1.0 / 128, None, ALU.mult, ALU.bypass))
        p.op("gpsimd", lambda e: e.tensor_scalar(m2[:], ssq[:], 1.0 / 128, None, ALU.mult, ALU.bypass))
        p.op("vector", lambda e: e.tensor_scalar(sct[:], plr, -1.0 / 128, None, ALU.mult, ALU.bypass))
        p.bar()
        p.op("vector", lambda e: e.scalar_tensor_tensor(u_[:], mu[:], mu[:, 0:1], m2[:], ALU.mult, ALU.subtract))
        p.op("gpsimd", lambda e: e.tensor_scalar(leta[:], sct[:], t15c[:, 0:1], None, ALU.mult, ALU.bypass))
        p.bar()
        p.op("scalar", lambda e: e.activation(stdv[:], u_[:], AF.Sqrt, bias=1e-6, scale=-1.0))
        p.op("vector", lambda e: e.tensor_copy(letab[:], leta[:]))
        p.bar()
        p.op("vector", lambda e: e.reciprocal(rstd[:], stdv[:]))
        p.bar()
        p.op("vector", lambda e: e.tensor_scalar(xh[:], pzA[:], mu[:, 0:1], rstd[:, 0:1], ALU.subtract, ALU.mult))
        p.bar()
        p.op("vector", lambda e: e.scalar_tensor_tensor(t1[:], xh[:], 1.0, gw2b[:], ALU.mult, ALU.mult, accum_out=st1[:]))
        p.bar()
        p.op("vector", lambda e: e.tensor_mul(u1[:], pgg[:], gwb16[:, 0:128]))
        p.bar()
        p.op("vector", lambda e: e.tensor_sub(gy[:], gbgwb[:], u1[:]))
        p.op("vector", lambda e: e.reduce_sum(sgg[:], u1[:], axis=AX.X))
        p.bar()
        p.op("vector", lambda e: e.tensor_add(gy[:], gy[:], t1[:]))
        p.op("gpsimd", lambda e: e.tensor_sub(mgy[:], st1[:], sgg[:]))
        p.bar()
        p.op("gpsimd", lambda e: e.tensor_scalar(mgy[:], mgy[:], sgbc[:, 0:1], 1.0 / 128, ALU.add, ALU.mult))
        p.bar()
        p.op("vector", lambda e: e.scalar_tensor_tensor(q5[:], gy[:], 1.0, xh[:], ALU.mult, ALU.mult, accum_out=sgx[:]))
        p.op("vector", lambda e, par=par: e.tensor_scalar(lk[:], pxk[:], leta[:, 0:1], None, ALU.mult, ALU.bypass))
        p.bar()
        p.op("gpsimd", lambda e: e.tensor_scalar(nmgyx[:], sgx[:], -1.0 / 128, None, ALU.mult, ALU.bypass))
        p.bar()
        p.op("vector", lambda e: e.scalar_tensor_tensor(aa[:], xh[:], nmgyx[:, 0:1], gy[:], ALU.mult, ALU.add))
        p.bar()
        p.op("vector", lambda e: e.tensor_scalar(grad[:], aa[:], mgy[:, 0:1], rstd[:, 0:1], ALU.subtract, ALU.mult))
        p.op("gpsimd", lambda e: e.tensor_scalar(nct[:], o1[:], sct[:, 0:1], None, ALU.mult, ALU.bypass))
        p.bar()
        p.op("tensor", lambda e: e.matmul(pzbar, nct[:], grad[:], start=False, stop=True, skip_group_check=True))
        p.op("tensor", lambda e: e.matmul(pw1[:], lk[:], grad[:], start=True, stop=True))
        p.op("tensor", lambda e: e.matmul(pb1, letab[:], grad[:], start=True, stop=True))
        p.bar()
        p.op("vector", lambda e, W1s_nxt=W1s_nxt, W1s_cur=W1s_cur: e.tensor_sub(W1s_nxt, W1s_cur, pw1[:]))
        p.op("scalar", lambda e, q=q: e.activation(zbb[:, q * 128:(q + 1) * 128], pzbar, AF.Copy, bias=0.0, scale=1.0))
        p.bar()
        p.op("vector", lambda e, b1_nxt=b1_nxt, b1_cur=b1_cur: e.tensor_sub(b1_nxt, b1_cur, pb1))
        p.op("vector", lambda e, W1b_nxt=W1b_nxt, W1s_nxt=W1s_nxt: e.tensor_copy(W1b_nxt, W1s_nxt))
        p.bar()
        p.op("scalar", lambda e, b1b_nxt=b1b_nxt, b1_nxt=b1_nxt: e.activation(b1b_nxt, b1_nxt, AF.Copy, bias=0.0, scale=1.0))
        p.bar()
        if q == 7:
            g0 = n - 7
            sview = zbb[:].rearrange("p (s n2) -> p s n2", s=8)
            p.op("vector", lambda e: e.reduce_sum(srow[:], sview, axis=AX.X))
            p.op("gpsimd", lambda e: e.tensor_mul(zb2[:], zbb[:], zbb[:]))
            p.bar()
            p.op("vector", lambda e: e.reduce_sum(sqrow[:], zb2[:].rearrange("p (s n2) -> p s n2", s=8), axis=AX.X))
            p.op("gpsimd", lambda e: e.tensor_scalar(murow[:], srow[:], 1.0 / 128, None, ALU.mult, ALU.bypass))
            p.bar()
            p.op("vector", lambda e: e.tensor_scalar(m2row[:], sqrow[:], 1.0 / 128, None, ALU.mult, ALU.bypass))
            p.bar()
            p.op("vector", lambda e: e.tensor_mul(urow[:], murow[:], murow[:]))
            p.bar()
            p.op("vector", lambda e: e.tensor_sub(urow[:], urow[:], m2row[:]))
            p.bar()
            p.op("scalar", lambda e: e.activation(stdrow[:], urow[:], AF.Sqrt, bias=1e-6, scale=-1.0))
            p.bar()
            p.op("vector", lambda e: e.reciprocal(rstdrow[:], stdrow[:]))
            p.bar()
            p.op("vector", lambda e: e.tensor_sub(
                zb2[:].rearrange("p (s n2) -> p s n2", s=8), sview,
                murow[:].rearrange("p s -> p s ()").broadcast_to((16, 8, 128))))
            p.bar()
            p.op("vector", lambda e: e.tensor_mul(
                zb2[:].rearrange("p (s n2) -> p s n2", s=8),
                zb2[:].rearrange("p (s n2) -> p s n2", s=8),
                rstdrow[:].rearrange("p s -> p s ()").broadcast_to((16, 8, 128))))
            p.bar()
            p.op("gpsimd", lambda e: e.tensor_add(xqtok[:], xqtok[:], gbb16[:]))
            p.bar()
            p.op("vector", lambda e: e.tensor_mul(zb2[:], zb2[:], gwb16[:]))
            p.bar()
            p.op("vector", lambda e: e.tensor_add(outc[:], zb2[:], xqtok[:]))
            p.bar()
            p.op("sync", lambda e, g0=g0: e.dma_start(
                out=outs_h[g0 * K:g0 * K + 128, :].rearrange("(j k) f -> k j f", k=16),
                in_=outc[:].rearrange("k (j f) -> k j f", j=8)), dma=True)
            p.bar()
    if dbg:
        ggd = c.sb([16, 128], F32, "ggd")
        p.op("vector", lambda e: e.tensor_copy(ggd[:], pgg[:]))
        p.bar()
        dbg_list = [("ggd", ggd), ("t1", t1), ("s_", s_), ("ssq", ssq), ("mu", mu), ("rstd", rstd), ("xh", xh),
                    ("st1", st1), ("sgg", sgg), ("gy", gy), ("mgy", mgy), ("sgx", sgx),
                    ("aa", aa), ("o1", o1), ("sct", sct), ("leta", leta),
                    ("masktokT", masktokT), ("t15c", t15c), ("tokrow_d", tokrow),
                    ("gw2b", gw2b), ("gbgwb", gbgwb), ("sgbc", sgbc), 
                    ("xqtok", xqtok), ("zbb", zbb), ("lrow_d", lrow[0:1, 0:min(256, L)])]
        for nm, t in dbg_list:
            o = c.out("dbg_" + nm, list(t.shape) if hasattr(t, 'shape') else [t.shape[0], t.shape[1]], F32)
            p.op("sync", lambda e, o=o, t=t: e.dma_start(out=o, in_=t[:] if hasattr(t, 'tensor_handle') or True else t), dma=True)
        for nm, t in [("grad", grad), ("nct", nct), ("lk", lk)]:
            o = c.out("dbg_" + nm, list(t.shape), BF16)
            p.op("sync", lambda e, o=o, t=t: e.dma_start(out=o, in_=t[:]), dma=True)
        for nm, t, sh in [("W1n", W1sb, [128, 256]), ("b1n", b1r, [1, 256])]:
            o = c.out("dbg_" + nm, sh, F32)
            p.op("sync", lambda e, o=o, t=t: e.dma_start(out=o, in_=t[:]), dma=True)
        p.bar()
    return c

def host_L2(inputs, hT_full, NM=512):
    K = 16
    L = NM * K
    maps = []
    identf = np.eye(128, dtype=np.float32)
    maskT = np.tril(np.ones((16, 16), np.float32)).T.copy()
    invc = (1.0 / np.arange(1, 17, dtype=np.float32)).reshape(16, 1)
    for b in range(2):
        for h in range(2):
            hT = np.ascontiguousarray(hT_full[:, b * 8192: b * 8192 + L])
            m = {
                "hT": hT,
                "wq": np.ascontiguousarray(np.asarray(inputs["L0_attn_wq"], np.float32)[:, h * 128:(h + 1) * 128]),
                "wk": np.ascontiguousarray(np.asarray(inputs["L0_attn_wk"], np.float32)[:, h * 128:(h + 1) * 128]),
                "wv": np.ascontiguousarray(np.asarray(inputs["L0_attn_wv"], np.float32)[:, h * 128:(h + 1) * 128]),
                "lrw": np.ascontiguousarray(np.asarray(inputs["L0_attn_lr_w"], np.float32)[h, 0, :].reshape(2, 128).T),
                "lrb": np.asarray(inputs["L0_attn_lr_b"], np.float32)[h].reshape(1, 1).copy(),
                "tokc": np.asarray(inputs["L0_attn_tok_idx"], np.float32).reshape(16, 1).copy(),
                "invc": invc,
                "maskT": maskT,
                "identf": identf,
                "W1": np.ascontiguousarray(np.asarray(inputs["L0_attn_W1"], np.float32)[h]),
                "b1": np.asarray(inputs["L0_attn_b1"], np.float32)[h].reshape(1, 128).copy(),
                "gw_row": np.asarray(inputs["L0_attn_ttt_ln_w"], np.float32)[h].reshape(1, 128).copy(),
                "gb_row": np.asarray(inputs["L0_attn_ttt_ln_b"], np.float32)[h].reshape(1, 128).copy(),
                "gw_col": np.asarray(inputs["L0_attn_ttt_ln_w"], np.float32)[h].reshape(128, 1).copy(),
            }
            maps.append(m)
    return maps


def numpy_scan_stream(inputs, h_f32, hh, NM):
    K, f = 16, 128
    L = NM * K
    h = h_f32[:L].astype(np.float64)
    wq = np.asarray(inputs["L0_attn_wq"], np.float64)[:, hh * 128:(hh + 1) * 128]
    wk = np.asarray(inputs["L0_attn_wk"], np.float64)[:, hh * 128:(hh + 1) * 128]
    wv = np.asarray(inputs["L0_attn_wv"], np.float64)[:, hh * 128:(hh + 1) * 128]
    lrw = np.asarray(inputs["L0_attn_lr_w"], np.float64)[hh, 0, :]
    lrb = np.asarray(inputs["L0_attn_lr_b"], np.float64)[hh]
    tok = np.maximum(1.0 / np.arange(1, 17) + np.asarray(inputs["L0_attn_tok_idx"], np.float64), 0.0)
    mask = np.tril(np.ones((16, 16)))
    W1 = np.asarray(inputs["L0_attn_W1"], np.float64)[hh].copy()
    b1 = np.asarray(inputs["L0_attn_b1"], np.float64)[hh].copy()
    g = np.asarray(inputs["L0_attn_ttt_ln_w"], np.float64)[hh][None, :]
    gbb = np.asarray(inputs["L0_attn_ttt_ln_b"], np.float64)[hh][None, :]
    XQ = (h @ wq).reshape(NM, K, f)
    XK = (h @ wk).reshape(NM, K, f)
    XV = (h @ wv).reshape(NM, K, f)
    lr = 1 / (1 + np.exp(-(h @ lrw + lrb))) / f
    outs = np.zeros((NM, K, f))
    for n in range(NM):
        xq, xk, xv = XQ[n], XK[n], XV[n]
        lr_col = lr[n * K:(n + 1) * K]
        Z = xk @ W1 + b1
        mu = Z.mean(-1, keepdims=True)
        var = ((Z - mu) ** 2).mean(-1, keepdims=True)
        rstd = 1.0 / np.sqrt(var + 1e-6)
        xh = (Z - mu) * rstd
        gy = (g * xh + gbb - (xv - xk)) * g
        grad = (gy - gy.mean(-1, keepdims=True) - xh * (gy * xh).mean(-1, keepdims=True)) * rstd
        Raw = xq @ xk.T
        CT = (Raw.T + 1.0) * mask.T * tok[None, :] * lr_col[:, None]
        Zbar = xq @ W1 + b1 - CT.T @ grad
        mu2 = Zbar.mean(-1, keepdims=True)
        var2 = ((Zbar - mu2) ** 2).mean(-1, keepdims=True)
        outs[n] = xq + g * (Zbar - mu2) / np.sqrt(var2 + 1e-6) + gbb
        leta = tok[K - 1] * lr_col
        W1 = W1 - (leta[:, None] * xk).T @ grad
        b1 = b1 - (leta[:, None] * grad).sum(0, keepdims=True)
    return outs.reshape(L, f)

def build_L3(upto=99):
    c = Ctx()
    T = TPC
    p = c.p
    o_h = c.inp("oT", [256, T], BF16)
    x_h = c.inp("xT", [D, T], F32)
    wo_h = c.inp("wo", [256, 256], F32)
    up_h = c.inp("up_w", [256, D], F32)
    pw_h = c.inp("post_w", [128, 2], F32)
    pb_h = c.inp("post_b", [128, 2], F32)
    lw_h = c.inp("ln_w", [128, 2], F32)
    lb_h = c.inp("ln_b", [128, 2], F32)
    ub_h = c.inp("up_b", [128, 8], F32)
    x1_o = c.out("x1T", [D, T], F32)
    c.setup_common(consts=[0.0, 1.0, 1e-5])
    ob = c.sb([128, 2 * T], BF16, "ob")
    for ch in range(2):
        p.op("sync", lambda e, ch=ch: e.dma_start(out=ob[:, ch * T:(ch + 1) * T], in_=o_h[ch * 128:(ch + 1) * 128, :]), dma=True)
    xT = c.sb([128, 8 * T], F32, "xT_sb")
    for ch in range(8):
        p.op("sync", lambda e, ch=ch: e.dma_start(out=xT[:, ch * T:(ch + 1) * T], in_=x_h[ch * 128:(ch + 1) * 128, :]), dma=True)
    wo = load_w_bf16(c, wo_h, 256, 256, "wo_sb")
    up = load_w_bf16(c, up_h, 256, D, "up_sb")
    small = {}
    for nm, hh, ncol in (("pw", pw_h, 2), ("pb", pb_h, 2), ("lw", lw_h, 2), ("lb", lb_h, 2), ("ub", ub_h, 8)):
        t = c.sb([128, ncol], F32, nm)
        p.op("sync", lambda e, t=t, hh=hh: e.dma_start(out=t[:], in_=hh[:]), dma=True)
        small[nm] = t
    p.bar()
    ps2 = c.ps([1, 1024], name="ps_stat")
    psA = c.ps([128, 512], name="psA")
    psB = c.ps([128, 512], name="psB")
    pbv = c.sb([128, 2], F32, "pbv")
    pb_bf = c.sb([128, 2], BF16, "pb_bf")
    p.op("vector", lambda e: e.tensor_copy(pb_bf[:], small["pb"][:]))
    p.bar()
    for fo in range(2):
        for ch in range(2):
            p.op("tensor", lambda e, fo=fo, ch=ch: e.matmul(
                psA[:, 0:1], wo[:, ch * 256 + fo * 128: ch * 256 + fo * 128 + 128],
                pb_bf[:, ch:ch + 1], start=(ch == 0), stop=(ch == 1)))
        p.bar()
        p.op("vector", lambda e, fo=fo: e.tensor_copy(pbv[:, fo:fo + 1], psA[:, 0:1]))
        p.bar()
    if upto < 1:
        dbg = c.out("o3T", [256, T], BF16)
        for ch in range(2):
            p.op("sync", lambda e, ch=ch: e.dma_start(out=dbg[ch * 128:(ch + 1) * 128, :], in_=ob[:, ch * T:(ch + 1) * T]), dma=True)
        p.bar()
        return c
    for ch in range(2):
        p.op("vector", lambda e, ch=ch: e.tensor_scalar(
            wo[:, ch * 256:(ch + 1) * 256], wo[:, ch * 256:(ch + 1) * 256],
            small["pw"][:, ch:ch + 1], None, ALU.mult, ALU.bypass))
    p.bar()
    st1 = ln_stats(c, ob, 256, T, 1e-5, ps2)
    negcs1 = colsum_neg_row(c, wo, 256, 256, ps2)
    rstd1bc = bcast_row(c, st1["rstd"], T, psA)
    o3 = c.sb([128, 2 * T], BF16, "o3")

    def cb1(fo, sl):
        return o3[:, fo * T + sl: fo * T + sl + 512]

    ln_proj(c, ob, 256, T, wo, 256, st1["mu"], rstd1bc, psA, cb1,
            bias_col=pbv, act=AF.Copy, negcs_f32=negcs1, wt_stride=256)
    if upto < 2:
        dbg = c.out("o3T", [256, T], BF16)
        for ch in range(2):
            p.op("sync", lambda e, ch=ch: e.dma_start(out=dbg[ch * 128:(ch + 1) * 128, :], in_=o3[:, ch * T:(ch + 1) * T]), dma=True)
        d2 = c.out("dbg_mu", [1, T], F32)
        d3 = c.out("dbg_rstd", [1, T], F32)
        d4 = c.out("dbg_negcs", [1, 256], F32)
        d5 = c.out("dbg_pbv", [128, 2], F32)
        p.op("sync", lambda e: e.dma_start(out=d2, in_=st1["mu"][:]), dma=True)
        p.op("sync", lambda e: e.dma_start(out=d3, in_=st1["rstd"][:]), dma=True)
        p.op("sync", lambda e: e.dma_start(out=d4, in_=negcs1[:]), dma=True)
        p.op("sync", lambda e: e.dma_start(out=d5, in_=pbv[:]), dma=True)
        p.bar()
        return c
    biasv = c.sb([128, 8], F32, "biasv")
    lb_bf = c.sb([128, 2], BF16, "lb_bf")
    p.op("vector", lambda e: e.tensor_copy(lb_bf[:], small["lb"][:]))
    p.bar()
    for fo in range(8):
        for ch in range(2):
            p.op("tensor", lambda e, fo=fo, ch=ch: e.matmul(
                psA[:, 0:1], up[:, ch * D + fo * 128: ch * D + fo * 128 + 128],
                lb_bf[:, ch:ch + 1], start=(ch == 0), stop=(ch == 1)))
        p.bar()
        p.op("vector", lambda e, fo=fo: e.tensor_add(biasv[:, fo:fo + 1], psA[:, 0:1], small["ub"][:, fo:fo + 1]))
        p.bar()
    for ch in range(2):
        p.op("vector", lambda e, ch=ch: e.tensor_scalar(
            up[:, ch * D:(ch + 1) * D], up[:, ch * D:(ch + 1) * D],
            small["lw"][:, ch:ch + 1], None, ALU.mult, ALU.bypass))
    p.bar()
    st2 = ln_stats(c, o3, 256, T, 1e-5, ps2)
    rstd2bc = bcast_row(c, st2["rstd"], T, psA)
    negcs2 = colsum_neg_row(c, up, 256, D, ps2)
    tmp = c.sb([128, 512], F32, "l3tmp")
    for fo in range(8):
        for sl in range(0, T, 512):
            for ch in range(2):
                p.op("tensor", lambda e, fo=fo, sl=sl, ch=ch: e.matmul(
                    psB[:, 0:512], up[:, ch * D + fo * 128: ch * D + fo * 128 + 128],
                    o3[:, ch * T + sl: ch * T + sl + 512], start=(ch == 0), stop=False))
            p.op("tensor", lambda e, fo=fo, sl=sl: e.matmul(
                psB[:, 0:512], negcs2[0:1, fo * 128:fo * 128 + 128],
                st2["mu"][0:1, sl:sl + 512], start=False, stop=True))
            p.bar()
            p.op("vector", lambda e, sl=sl: e.tensor_mul(tmp[:], psB[:, 0:512], rstd2bc[:, sl:sl + 512]))
            p.bar()
            p.op("vector", lambda e, fo=fo, sl=sl: e.scalar_tensor_tensor(
                xT[:, fo * T + sl: fo * T + sl + 512], tmp[:], biasv[:, fo:fo + 1],
                xT[:, fo * T + sl: fo * T + sl + 512], ALU.add, ALU.add))
            p.bar()
    for ch in range(8):
        p.op("sync", lambda e, ch=ch: e.dma_start(out=x1_o[ch * 128:(ch + 1) * 128, :], in_=xT[:, ch * T:(ch + 1) * T]), dma=True)
    p.bar()
    return c


def host_L3(inputs, outs_streams, x2d):
    o_full = np.zeros((TOK, 256), dtype=BF)
    for b in range(2):
        for h in range(2):
            o_full[b * 8192:(b + 1) * 8192, h * 128:(h + 1) * 128] = outs_streams[(b, h)]
    xTs = shard_xT(x2d)
    maps = []
    for i in range(NCORE):
        maps.append({
            "oT": np.ascontiguousarray(o_full[i * TPC:(i + 1) * TPC].T),
            "xT": xTs[i],
            "wo": np.ascontiguousarray(np.asarray(inputs["L0_attn_wo"], np.float32)),
            "up_w": np.ascontiguousarray(np.asarray(inputs["L0_attn_up_w"], np.float32)),
            "post_w": np.ascontiguousarray(np.asarray(inputs["L0_attn_post_w"], np.float32).reshape(2, 128).T),
            "post_b": np.ascontiguousarray(np.asarray(inputs["L0_attn_post_b"], np.float32).reshape(2, 128).T),
            "ln_w": np.ascontiguousarray(np.asarray(inputs["L0_attn_ln_w"], np.float32).reshape(2, 128).T),
            "ln_b": np.ascontiguousarray(np.asarray(inputs["L0_attn_ln_b"], np.float32).reshape(2, 128).T),
            "up_b": np.ascontiguousarray(np.asarray(inputs["L0_attn_up_b"], np.float32).reshape(8, 128).T),
        })
    return maps

def build_L4():
    c = Ctx()
    T = TPC
    p = c.p
    x_h = c.inp("x1T", [D, T], F32)
    wT_h = c.inp("WT", [D, D], F32)
    ldT_h = c.inp("ldT", [100, 4], F32)
    da_h = c.inp("da", [100, D], F32)
    luT_h = c.inp("luT", [4, 50], F32)
    uaT_h = c.inp("uaT", [50, D], F32)
    b_h = c.inp("bcol", [128, 8], F32)
    x2_o = c.out("x2T", [D, T], F32)
    part_o = c.out("part", [128, 8], F32)
    r2_o = c.out("r2row", [2, T], F32)
    c.setup_common(consts=[0.0, 1.0, 1e-5])
    xT = c.sb([128, 8 * T], F32, "xT_sb")
    xb = c.sb([128, 8 * T], BF16, "xb")
    for ch in range(8):
        p.op("sync", lambda e, ch=ch: e.dma_start(out=xT[:, ch * T:(ch + 1) * T], in_=x_h[ch * 128:(ch + 1) * 128, :]), dma=True)
        p.op("gpsimd", lambda e, ch=ch: e.dma_start(out=xb[:, ch * T:(ch + 1) * T], in_=x_h[ch * 128:(ch + 1) * 128, :]), dma=True)
    weff = c.sb([128, 8 * D], BF16, "weff")
    ldT = c.sb([100, 4], BF16, "ldT_sb")
    da = c.sb([100, D], BF16, "da_sb")
    luT = c.sb([4, 50], BF16, "luT_sb")
    uaT = c.sb([50, D], BF16, "uaT_sb")
    bcol = c.sb([128, 8], F32, "bcol_sb")
    for t, hh in ((ldT, ldT_h), (da, da_h), (luT, luT_h), (uaT, uaT_h)):
        p.op("gpsimd", lambda e, t=t, hh=hh: e.dma_start(out=t[:], in_=hh[:]), dma=True)
    p.op("sync", lambda e: e.dma_start(out=bcol[:], in_=b_h[:]), dma=True)
    p.bar()
    ps2 = c.ps([1, 1024], name="ps_stat")
    psA = c.ps([128, 512], name="psA")
    psB = c.ps([128, 512], name="psB")
    psC = c.ps([128, 1024], name="psC")
    t1 = c.sb([4, D], BF16, "t1_li")
    for sl in range(0, D, 512):
        p.op("tensor", lambda e, sl=sl: e.matmul(psA[0:4, 0:512], ldT[:], da[:, sl:sl + 512], start=True, stop=True))
        p.bar()
        p.op("vector", lambda e, sl=sl: e.tensor_copy(t1[:, sl:sl + 512], psA[0:4, 0:512]))
        p.bar()
    t2 = c.sb([50, D], BF16, "t2_li")
    for sl in range(0, D, 512):
        p.op("tensor", lambda e, sl=sl: e.matmul(psA[0:50, 0:512], luT[:], t1[:, sl:sl + 512], start=True, stop=True))
        p.bar()
        p.op("vector", lambda e, sl=sl: e.tensor_copy(t2[:, sl:sl + 512], psA[0:50, 0:512]))
        p.bar()
    for ch in range(8):
        p.op("gpsimd", lambda e, ch=ch: e.dma_start(out=weff[:, ch * D:(ch + 1) * D], in_=wT_h[ch * 128:(ch + 1) * 128, :]), dma=True)
    p.bar()
    for ch in range(8):
        for sl in range(0, D, 512):
            p.op("tensor", lambda e, ch=ch, sl=sl: e.matmul(
                psC[:, 0:512], t2[:, ch * 128:(ch + 1) * 128], uaT[:, sl:sl + 512], start=True, stop=True))
            p.bar()
            p.op("vector", lambda e, ch=ch, sl=sl: e.tensor_add(
                weff[:, ch * D + sl: ch * D + sl + 512], weff[:, ch * D + sl: ch * D + sl + 512], psC[:, 0:512]))
            p.bar()
    st = ln_stats(c, xb, D, T, 1e-5, ps2)
    negcs = colsum_neg_row(c, weff, D, D, ps2)
    rstd_bc = bcast_row(c, st["rstd"], T, psA)
    tmp = c.sb([128, 512], F32, "l4tmp")
    x2b = c.sb([128, 8 * T], BF16, "x2b")
    for fo in range(8):
        for sl in range(0, T, 512):
            for ch in range(8):
                p.op("tensor", lambda e, fo=fo, sl=sl, ch=ch: e.matmul(
                    psB[:, 0:512], weff[:, ch * D + fo * 128: ch * D + fo * 128 + 128],
                    xb[:, ch * T + sl: ch * T + sl + 512], start=(ch == 0), stop=False))
            p.op("tensor", lambda e, fo=fo, sl=sl: e.matmul(
                psB[:, 0:512], negcs[0:1, fo * 128:fo * 128 + 128],
                st["mu"][0:1, sl:sl + 512], start=False, stop=True))
            p.bar()
            p.op("vector", lambda e, sl=sl: e.tensor_mul(tmp[:], psB[:, 0:512], rstd_bc[:, sl:sl + 512]))
            p.bar()
            p.op("vector", lambda e, fo=fo, sl=sl: e.scalar_tensor_tensor(
                xT[:, fo * T + sl: fo * T + sl + 512], tmp[:], bcol[:, fo:fo + 1],
                xT[:, fo * T + sl: fo * T + sl + 512], ALU.add, ALU.add))
            p.bar()
            p.op("vector", lambda e, fo=fo, sl=sl: e.tensor_copy(
                x2b[:, fo * T + sl: fo * T + sl + 512], xT[:, fo * T + sl: fo * T + sl + 512]))
            p.bar()
    st2 = ln_stats(c, x2b, D, T, 1e-5, ps2, rows=st)
    rstd2bc = bcast_row(c, st2["rstd"], T, psA, out=rstd_bc)
    part = c.sb([128, 8], F32, "part_sb")
    scr = c.sb([128, 512], F32, "l4scr")
    pcol = c.sb([128, 1], F32, "pcol")
    smur = c.sb([1, 1], F32, "smur")
    p.op("vector", lambda e: e.reduce_sum(smur[:], st2["mur"][:], axis=AX.X))
    p.bar()
    smurc = c.sb([128, 1], F32, "smurc")
    p.op("tensor", lambda e: e.matmul(psA[:, 0:1], c.ones_row[:], smur[:], start=True, stop=True))
    p.bar()
    p.op("vector", lambda e: e.tensor_copy(smurc[:], psA[:, 0:1]))
    p.bar()
    for ch in range(8):
        p.op("vector", lambda e: e.memset(part[:, 0:1], 0.0)) if False else None
        for sl in range(0, T, 512):
            p.op("vector", lambda e, ch=ch, sl=sl: e.scalar_tensor_tensor(
                scr[:], x2b[:, ch * T + sl: ch * T + sl + 512], 1.0,
                rstd2bc[:, sl:sl + 512], ALU.mult, ALU.mult, accum_out=pcol[:]))
            p.bar()
            if sl == 0:
                p.op("vector", lambda e, ch=ch: e.tensor_copy(part[:, ch:ch + 1], pcol[:]))
            else:
                p.op("vector", lambda e, ch=ch: e.tensor_add(part[:, ch:ch + 1], part[:, ch:ch + 1], pcol[:]))
            p.bar()
        p.op("vector", lambda e, ch=ch: e.tensor_sub(part[:, ch:ch + 1], part[:, ch:ch + 1], smurc[:]))
        p.bar()
    for ch in range(8):
        p.op("sync", lambda e, ch=ch: e.dma_start(out=x2_o[ch * 128:(ch + 1) * 128, :], in_=xT[:, ch * T:(ch + 1) * T]), dma=True)
    p.op("sync", lambda e: e.dma_start(out=part_o, in_=part[:]), dma=True)
    p.op("sync", lambda e: e.dma_start(out=r2_o[0:1, :], in_=st2["mu"][:]), dma=True)
    p.op("sync", lambda e: e.dma_start(out=r2_o[1:2, :], in_=st2["rstd"][:]), dma=True)
    p.bar()
    return c


def host_L4(inputs, x1_2d):
    xTs = shard_xT(x1_2d)
    WT = np.ascontiguousarray(np.asarray(inputs["L0_ffn_W"], np.float32).T)
    ldT = np.ascontiguousarray(np.asarray(inputs["L0_ffn_lora_down"], np.float32).T)
    da = np.ascontiguousarray(np.asarray(inputs["L0_ffn_down_aux"], np.float32))
    luT = np.ascontiguousarray(np.asarray(inputs["L0_ffn_lora_up"], np.float32).T)
    uaT = np.ascontiguousarray(np.asarray(inputs["L0_ffn_up_aux"], np.float32).T)
    bcol = np.ascontiguousarray(np.asarray(inputs["L0_ffn_b"], np.float32).reshape(8, 128).T)
    return [{"x1T": xTs[i], "WT": WT, "ldT": ldT, "da": da, "luT": luT, "uaT": uaT, "bcol": bcol}
            for i in range(NCORE)]

def build_L5():
    c = Ctx()
    T = TPC
    HL = 40
    p = c.p
    x_h = c.inp("x2T", [D, T], F32)
    xh_h = c.inp("xhT", [D, HL], F32)
    r2_h = c.inp("r2row", [2, T], F32)
    r2h_h = c.inp("r2h", [2, HL], F32)
    hv_h = c.inp("hvalid", [128, 1], F32)
    parts_h = [c.inp(f"part{i}", [128, 8], F32) for i in range(4)]
    w1_h = c.inp("W1", [D, D], F32)
    w2_h = c.inp("W2", [D, D], F32)
    mw_h = c.inp("mwT", [D, 41], F32)
    b1_h = c.inp("b1col", [128, 8], F32)
    b2_h = c.inp("b2col", [128, 8], F32)
    x3_o = c.out("x3T", [D, T], F32)
    r3_o = c.out("r3row", [2, T], F32)
    c.setup_common(consts=[0.0, 1.0, 1e-5, 8192.0])
    xb = c.sb([128, 8 * T], BF16, "xb")
    for ch in range(8):
        p.op("gpsimd", lambda e, ch=ch: e.dma_start(out=xb[:, ch * T:(ch + 1) * T], in_=x_h[ch * 128:(ch + 1) * 128, :]), dma=True)
    xhb = c.sb([128, 8 * HL], BF16, "xhb")
    for ch in range(8):
        p.op("gpsimd", lambda e, ch=ch: e.dma_start(out=xhb[:, ch * HL:(ch + 1) * HL], in_=xh_h[ch * 128:(ch + 1) * 128, :]), dma=True)
    w1 = load_w_bf16(c, w1_h, D, D, "w1_sb")
    w2 = load_w_bf16(c, w2_h, D, D, "w2_sb")
    mwT = c.sb([128, 8 * 41], F32, "mwT_sb")
    for ch in range(8):
        p.op("sync", lambda e, ch=ch: e.dma_start(out=mwT[:, ch * 41:(ch + 1) * 41], in_=mw_h[ch * 128:(ch + 1) * 128, :]), dma=True)
    smalls = {}
    for nm, hh in (("hv", hv_h), ("b1", b1_h), ("b2", b2_h)):
        shp = [128, 1] if nm == "hv" else [128, 8]
        t = c.sb(shp, F32, nm + "_sb")
        p.op("sync", lambda e, t=t, hh=hh: e.dma_start(out=t[:], in_=hh[:]), dma=True)
        smalls[nm] = t
    for nm, hh, w in (("r2mu", r2_h, T), ("r2rs", r2_h, T), ("r2hmu", r2h_h, HL), ("r2hrs", r2h_h, HL)):
        t = c.sb([1, w], F32, nm + "_sb")
        row = 0 if nm.endswith("mu") else 1
        p.op("sync", lambda e, t=t, hh=hh, row=row: e.dma_start(out=t[:], in_=hh[row:row + 1, :]), dma=True)
        smalls[nm] = t
    parts = []
    for i in range(4):
        t = c.sb([128, 8], F32, f"part{i}_sb")
        p.op("sync", lambda e, t=t, hh=parts_h[i]: e.dma_start(out=t[:], in_=hh[:]), dma=True)
        parts.append(t)
    p.bar()
    ps2 = c.ps([1, 1024], name="ps_stat")
    psA = c.ps([128, 512], name="psA")
    psB = c.ps([128, 512], name="psB")
    psH = c.ps([128, 40], name="psH")
    tapT = c.sb([128, 8 * 40], F32, "tapT")
    for ch in range(8):
        p.op("vector", lambda e, ch=ch: e.scalar_tensor_tensor(
            tapT[:, ch * 40:(ch + 1) * 40], mwT[:, ch * 41 + 1:(ch + 1) * 41], 1.0,
            mwT[:, ch * 41:ch * 41 + 1].broadcast_to((128, 40)), ALU.mult, ALU.subtract))
    p.bar()
    PW = HL + T
    pT = c.sb([128, 8 * PW], BF16, "pT")
    negcs = colsum_neg_row(c, w1, D, D, ps2)
    mu_r = smalls["r2mu"][:]
    rstd_r = smalls["r2rs"][:]
    rstd_bc = bcast_row(c, rstd_r, T, psA)
    tmp = c.sb([128, 512], F32, "l5tmp")
    for fo in range(8):
        for sl in range(0, T, 512):
            for ch in range(8):
                p.op("tensor", lambda e, fo=fo, sl=sl, ch=ch: e.matmul(
                    psB[:, 0:512], w1[:, ch * D + fo * 128: ch * D + fo * 128 + 128],
                    xb[:, ch * T + sl: ch * T + sl + 512], start=(ch == 0), stop=False))
            p.op("tensor", lambda e, fo=fo, sl=sl: e.matmul(
                psB[:, 0:512], negcs[0:1, fo * 128:fo * 128 + 128],
                mu_r[0:1, sl:sl + 512], start=False, stop=True))
            p.bar()
            p.op("vector", lambda e, sl=sl: e.tensor_mul(tmp[:], psB[:, 0:512], rstd_bc[:, sl:sl + 512]))
            p.bar()
            p.op("vector", lambda e, fo=fo, sl=sl: e.tensor_scalar(
                pT[:, fo * PW + HL + sl: fo * PW + HL + sl + 512], tmp[:],
                smalls["b1"][:, fo:fo + 1], None, ALU.add, ALU.bypass))
            p.bar()
    rstdh_bc = c.sb([128, HL], F32, "rstdh_bc")
    p.op("tensor", lambda e: e.matmul(psH[:], c.ones_row[:], smalls["r2hrs"][:], start=True, stop=True))
    p.bar()
    p.op("vector", lambda e: e.tensor_copy(rstdh_bc[:], psH[:]))
    p.bar()
    for fo in range(8):
        for ch in range(8):
            p.op("tensor", lambda e, fo=fo, ch=ch: e.matmul(
                psH[:], w1[:, ch * D + fo * 128: ch * D + fo * 128 + 128],
                xhb[:, ch * HL:(ch + 1) * HL], start=(ch == 0), stop=False))
        p.op("tensor", lambda e, fo=fo: e.matmul(
            psH[:], negcs[0:1, fo * 128:fo * 128 + 128],
            smalls["r2hmu"][:], start=False, stop=True))
        p.bar()
        p.op("vector", lambda e: e.tensor_mul(tmp[:, 0:HL], psH[:], rstdh_bc[:]))
        p.bar()
        p.op("vector", lambda e, fo=fo: e.tensor_scalar(
            tmp[:, 64:64 + HL], tmp[:, 0:HL], smalls["b1"][:, fo:fo + 1], None, ALU.add, ALU.bypass))
        p.bar()
        p.op("vector", lambda e, fo=fo: e.tensor_scalar(
            pT[:, fo * PW: fo * PW + HL], tmp[:, 64:64 + HL], smalls["hv"][:, 0:1], None, ALU.mult, ALU.bypass))
        p.bar()
    accV = c.sb([128, T], F32, "accV")
    accG = c.sb([128, T], F32, "accG")
    lnsum = c.sb([128, 8], F32, "lnsum")
    lnsumb = c.sb([128, 8], BF16, "lnsumb")
    totcol = c.sb([128, 8], F32, "totcol")
    b18 = c.sb([128, 8], F32, "b18")
    mtcol = c.sb([128, 8], F32, "mtcol")
    p.op("vector", lambda e: e.tensor_add(lnsum[:], parts[0][:], parts[1][:]))
    p.op("gpsimd", lambda e: e.tensor_add(totcol[:], parts[2][:], parts[3][:]))
    p.op("scalar", lambda e: e.mul(b18[:], smalls["b1"][:], 8192.0))
    p.bar()
    p.op("vector", lambda e: e.tensor_add(lnsum[:], lnsum[:], totcol[:]))
    p.bar()
    p.op("vector", lambda e: e.tensor_copy(lnsumb[:], lnsum[:]))
    p.bar()
    for ot in range(8):
        for ch in range(8):
            p.op("tensor", lambda e, ot=ot, ch=ch: e.matmul(
                psA[:, 0:1], w1[:, ch * D + ot * 128: ch * D + ot * 128 + 128],
                lnsumb[:, ch:ch + 1], start=(ch == 0), stop=(ch == 7)))
        p.bar()
        p.op("vector", lambda e, ot=ot: e.tensor_add(totcol[:, ot:ot + 1], psA[:, 0:1], b18[:, ot:ot + 1]))
        p.bar()
    for ch in range(8):
        p.op("vector", lambda e, ch=ch: e.tensor_mul(mtcol[:, ch:ch + 1], mwT[:, ch * 41:ch * 41 + 1], totcol[:, ch:ch + 1]))
    p.bar()
    tmpG = c.sb([128, T], F32, "tmpG")
    for ch in range(8):
        for j in range(26):
            srcV = pT[:, ch * PW + (HL - j): ch * PW + (HL - j) + T]
            tapV = tapT[:, ch * 40 + j: ch * 40 + j + 1]
            jg = 26 + j if j < 14 else None
            if jg is not None:
                srcG = pT[:, ch * PW + (HL - jg): ch * PW + (HL - jg) + T]
                tapG = tapT[:, ch * 40 + jg: ch * 40 + jg + 1]
                p.op("gpsimd", lambda e, srcG=srcG, tapG=tapG: e.tensor_scalar(
                    tmpG[:], srcG, tapG, None, ALU.mult, ALU.bypass))
            if j == 0:
                p.op("vector", lambda e, srcV=srcV, tapV=tapV: e.tensor_scalar(
                    accV[:], srcV, tapV, None, ALU.mult, ALU.bypass))
            else:
                p.op("vector", lambda e, srcV=srcV, tapV=tapV: e.scalar_tensor_tensor(
                    accV[:], srcV, tapV, accV[:], ALU.mult, ALU.add))
            p.bar()
            if jg is not None:
                if j == 0:
                    p.op("gpsimd", lambda e: e.tensor_copy(accG[:], tmpG[:]))
                else:
                    p.op("gpsimd", lambda e: e.tensor_add(accG[:], accG[:], tmpG[:]))
                p.bar()
        p.op("vector", lambda e, ch=ch: e.scalar_tensor_tensor(
            accV[:], accV[:], mtcol[:, ch:ch + 1], accG[:], ALU.add, ALU.add))
        p.bar()
        p.op("vector", lambda e, ch=ch: e.tensor_copy(pT[:, ch * PW + HL:(ch + 1) * PW], accV[:]))
        p.bar()
    x3b = xb
    xbuf = c.sb([128, 512], F32, "xbuf")
    x3row = c.sb([128, T], F32, "x3row")
    for fo in range(8):
        for sl in range(0, T, 512):
            p.op("sync", lambda e, fo=fo, sl=sl: e.dma_start(
                out=xbuf[:], in_=x_h[fo * 128:(fo + 1) * 128, sl:sl + 512]), dma=True)
            for ch in range(8):
                p.op("tensor", lambda e, fo=fo, sl=sl, ch=ch: e.matmul(
                    psB[:, 0:512], w2[:, ch * D + fo * 128: ch * D + fo * 128 + 128],
                    pT[:, ch * PW + HL + sl: ch * PW + HL + sl + 512], start=(ch == 0), stop=(ch == 7)))
            p.bar()
            p.op("vector", lambda e, fo=fo, sl=sl: e.scalar_tensor_tensor(
                x3row[:, sl:sl + 512], psB[:, 0:512], smalls["b2"][:, fo:fo + 1],
                xbuf[:], ALU.add, ALU.add))
            p.bar()
            p.op("vector", lambda e, fo=fo, sl=sl: e.tensor_copy(
                x3b[:, fo * T + sl: fo * T + sl + 512], x3row[:, sl:sl + 512]))
            p.bar()
        p.op("sync", lambda e, fo=fo: e.dma_start(out=x3_o[fo * 128:(fo + 1) * 128, :], in_=x3row[:]), dma=True)
        p.bar()
    st3 = ln_stats(c, x3b, D, T, 1e-5, ps2)
    p.op("sync", lambda e: e.dma_start(out=r3_o[0:1, :], in_=st3["mu"][:]), dma=True)
    p.op("sync", lambda e: e.dma_start(out=r3_o[1:2, :], in_=st3["rstd"][:]), dma=True)
    p.bar()
    return c


def host_L5(inputs, x2_2d, r2rows, parts):
    xTs = shard_xT(x2_2d)
    W1 = np.ascontiguousarray(np.asarray(inputs["L1_attn_W1"], np.float32))
    W2 = np.ascontiguousarray(np.asarray(inputs["L1_attn_W2"], np.float32))
    mwT = np.ascontiguousarray(np.asarray(inputs["L1_attn_memory_weights"], np.float32).T)
    b1c = np.ascontiguousarray(np.asarray(inputs["L1_attn_bias1"], np.float32).reshape(8, 128).T)
    b2c = np.ascontiguousarray(np.asarray(inputs["L1_attn_bias2"], np.float32).reshape(8, 128).T)
    maps = []
    for i in range(NCORE):
        grp = i // 4
        first = (i % 4 == 0)
        if first:
            xh = np.zeros((D, 40), np.float32)
            r2h = np.zeros((2, 40), np.float32)
            hv = np.zeros((128, 1), np.float32)
        else:
            xh = np.ascontiguousarray(xTs[i - 1][:, -40:])
            r2h = np.ascontiguousarray(r2rows[i - 1][:, -40:])
            hv = np.ones((128, 1), np.float32)
        m = {"x2T": xTs[i], "xhT": xh, "r2row": r2rows[i], "r2h": r2h, "hvalid": hv,
             "W1": W1, "W2": W2, "mwT": mwT, "b1col": b1c, "b2col": b2c}
        for k in range(4):
            m[f"part{k}"] = parts[grp * 4 + k]
        maps.append(m)
    return maps

def build_L6(dbg=False):
    c = Ctx()
    S = 8192
    p = c.p
    x_h = c.inp("x3T", [D, S], F32)
    rmu_h = c.inp("r3mu", [1, S], F32)
    rrs_h = c.inp("r3rs", [1, S], F32)
    qS_h = c.inp("qS", [D, 256], F32)
    kS_h = c.inp("kS", [D, 256], F32)
    vS_h = c.inp("vS", [D, 256], F32)
    qb_h = c.inp("qb", [128, 2], F32)
    kb_h = c.inp("kb", [128, 2], F32)
    vb_h = c.inp("vbrow", [1, 256], F32)
    cos_h = c.inp("cosT", [128, S], BF16)
    sin_h = c.inp("sinsT", [128, S], BF16)
    ao_h = c.out("aoT", [256, S], BF16)
    c.setup_common(consts=[0.0, 1.0, -24.0, 0.125])
    QT = c.sb([128, 2 * S], BF16, "QT")
    KT = c.sb([128, 2 * S], BF16, "KT")
    VT = c.sb([128, 16384], BF16, "VT")
    cosT = c.sb([128, S], BF16, "cosT_sb")
    sinsT = c.sb([128, S], BF16, "sinsT_sb")
    p.op("sync", lambda e: e.dma_start(out=cosT[:], in_=cos_h[:]), dma=True)
    p.op("sync", lambda e: e.dma_start(out=sinsT[:], in_=sin_h[:]), dma=True)
    qS = load_w_bf16(c, qS_h, D, 256, "qS_sb")
    kS = load_w_bf16(c, kS_h, D, 256, "kS_sb")
    vS = load_w_bf16(c, vS_h, D, 256, "vS_sb")
    qb = c.sb([128, 2], F32, "qb_sb")
    kb = c.sb([128, 2], F32, "kb_sb")
    vbr = c.sb([1, 256], F32, "vbr_sb")
    p.op("sync", lambda e: e.dma_start(out=qb[:], in_=qb_h[:]), dma=True)
    p.op("sync", lambda e: e.dma_start(out=kb[:], in_=kb_h[:]), dma=True)
    p.op("sync", lambda e: e.dma_start(out=vbr[:], in_=vb_h[:]), dma=True)
    p.bar()
    P1 = c.ps([128, 512], name="P1")
    P2 = c.ps([128, 512], name="P2")
    A1 = c.ps([64, 1024], name="A1")
    DEN = c.ps([1, 1024], name="DEN")
    BC = c.ps([64, 1024], name="BC")
    ps2 = DEN
    psA = P1
    psV = P2[:, 0:256]
    psVB = P2[:, 256:512]
    psRC = P1[:, 0:1]
    c._l6ps = (P1, P2, A1, DEN, BC)
    negq = colsum_neg_row(c, qS, D, 256, ps2)
    negk = colsum_neg_row(c, kS, D, 256, ps2)
    negv = colsum_neg_row(c, vS, D, 256, ps2)
    for half in range(2):
        p.op("tensor", lambda e, half=half: e.matmul(
            psVB[:, half * 128:(half + 1) * 128] if False else P2[:, 256 + half * 128: 256 + (half + 1) * 128],
            c.ones_row[:], vbr[0:1, half * 128:(half + 1) * 128],
            start=True, stop=True))
    p.bar()
    TS = 1024
    xsb = c.sb([128, 8 * TS], BF16, "xsb")
    murow = c.sb([1, TS], F32, "murow")
    rsrow = c.sb([1, TS], F32, "rsrow")
    mur = c.sb([1, TS], F32, "mur_r")
    rstd_bc = c.sb([128, TS], F32, "rstd_bc6")
    tmp = c.sb([128, 512], F32, "l6tmp")
    tmpv = c.sb([128, 256], F32, "l6tmpv")
    rcol = c.sb([128, 1], F32, "rcol")
    rt = c.sb([128, TS], BF16, "rt_rope")
    for tsl in range(8):
        t0 = tsl * TS
        for ch in range(8):
            p.op("gpsimd", lambda e, ch=ch, t0=t0: e.dma_start(
                out=xsb[:, ch * TS:(ch + 1) * TS], in_=x_h[ch * 128:(ch + 1) * 128, t0:t0 + TS]), dma=True)
        p.op("sync", lambda e, t0=t0: e.dma_start(out=murow[:], in_=rmu_h[0:1, t0:t0 + TS]), dma=True)
        p.op("sync", lambda e, t0=t0: e.dma_start(out=rsrow[:], in_=rrs_h[0:1, t0:t0 + TS]), dma=True)
        p.bar()
        p.op("vector", lambda e: e.tensor_mul(mur[:], murow[:], rsrow[:]))
        p.bar()
        for sl in range(0, TS, 512):
            p.op("tensor", lambda e, sl=sl: e.matmul(psA[:, 0:512], c.ones_row[:], rsrow[0:1, sl:sl + 512], start=True, stop=True))
            p.bar()
            p.op("vector", lambda e, sl=sl: e.tensor_copy(rstd_bc[:, sl:sl + 512], psA[:, 0:512]))
            p.bar()
        for (dst, wt, ng, bcol) in ((QT, qS, negq, qb), (KT, kS, negk, kb)):
            for pair in range(2):
                for sl in range(0, TS, 512):
                    for ch in range(8):
                        p.op("tensor", lambda e, wt=wt, pair=pair, sl=sl, ch=ch: e.matmul(
                            psA[:, 0:512], wt[:, ch * 256 + pair * 128: ch * 256 + pair * 128 + 128],
                            xsb[:, ch * TS + sl: ch * TS + sl + 512], start=(ch == 0), stop=False))
                    p.op("tensor", lambda e, ng=ng, pair=pair, sl=sl: e.matmul(
                        psA[:, 0:512], ng[0:1, pair * 128:pair * 128 + 128],
                        murow[0:1, sl:sl + 512], start=False, stop=True))
                    p.bar()
                    p.op("vector", lambda e, sl=sl: e.tensor_mul(tmp[:], psA[:, 0:512], rstd_bc[:, sl:sl + 512]))
                    p.bar()
                    p.op("vector", lambda e, dst=dst, bcol=bcol, pair=pair, sl=sl, t0=t0: e.tensor_scalar(
                        dst[:, pair * S + t0 + sl: pair * S + t0 + sl + 512], tmp[:],
                        bcol[:, pair:pair + 1], None, ALU.add, ALU.bypass))
                    p.bar()
        for dst in (QT, KT):
            for pair in range(2):
                base = pair * S + t0
                seg = dst[:, base: base + TS]
                for r0 in range(0, 128, 32):
                    src_r = (r0 + 32) if (r0 % 64 == 0) else (r0 - 32)
                    p.op("vector", lambda e, seg=seg, r0=r0, src_r=src_r: e.tensor_copy(
                        rt[r0:r0 + 32, :], seg[src_r:src_r + 32, :]))
                p.bar()
                p.op("vector", lambda e, t0=t0: e.tensor_mul(rt[:], rt[:], sinsT[:, t0:t0 + TS]))
                p.bar()
                p.op("vector", lambda e, seg=seg, t0=t0: e.tensor_mul(seg, seg, cosT[:, t0:t0 + TS]))
                p.bar()
                p.op("vector", lambda e, seg=seg: e.tensor_add(seg, seg, rt[:]))
                p.bar()
        for tt in range(8):
            gt = tsl * 8 + tt
            for ch in range(8):
                p.op("tensor", lambda e, ch=ch, tt=tt: e.matmul(
                    psV, xsb[:, ch * TS + tt * 128: ch * TS + tt * 128 + 128],
                    vS[:, ch * 256:(ch + 1) * 256], start=(ch == 0), stop=False))
            p.op("tensor", lambda e, tt=tt: e.matmul(
                psV, murow[0:1, tt * 128:tt * 128 + 128], negv[:], start=False, stop=True))
            p.bar()
            p.op("tensor", lambda e, tt=tt: e.transpose(psRC, rsrow[0:1, tt * 128:tt * 128 + 128], c.ones_row[0:1, 0:1]))
            p.bar()
            p.op("vector", lambda e: e.tensor_copy(rcol[:], psRC))
            p.bar()
            p.op("vector", lambda e: e.tensor_scalar(tmpv[:], psV, rcol[:, 0:1], None, ALU.mult, ALU.bypass))
            p.bar()
            p.op("vector", lambda e, gt=gt: e.tensor_add(VT[:, gt * 256:(gt + 1) * 256], tmpv[:], psVB))
            p.bar()
    if dbg:
        dq = c.out("dbg_QT", [128, 2 * S], BF16)
        dk = c.out("dbg_KT", [128, 2 * S], BF16)
        dv = c.out("dbg_VT", [128, 16384], BF16)
        p.op("sync", lambda e: e.dma_start(out=dq, in_=QT[:]), dma=True)
        p.op("sync", lambda e: e.dma_start(out=dk, in_=KT[:]), dma=True)
        p.op("sync", lambda e: e.dma_start(out=dv, in_=VT[:]), dma=True)
        p.bar()
    return c, (QT, KT, VT, ao_h)


def build_L6_attn(c, handles):
    QT, KT, VT, ao_h = handles
    S = 8192
    p = c.p
    P1, P2, A1, DEN, BC = c._l6ps
    psACC = A1
    psDEN = DEN
    psBC = BC
    PT = c.sb([128, 1024], BF16, "PT")
    accsb = c.sb([64, 1024], F32, "accsb")
    recr = c.sb([1, 1024], F32, "recr")
    abuf = c.sb([64, 1024], BF16, "abuf")
    for h in range(4):
        pair = h // 2
        r0 = (h % 2) * 64
        for qb_ in range(8):
            q0 = qb_ * 1024
            for kc in range(64):
                for half in range(2):
                    ph = P1 if half == 0 else P2
                    p.op("tensor", lambda e, pair=pair, r0=r0, kc=kc, q0=q0, half=half, ph=ph: e.matmul(
                        ph[:, 0:512],
                        KT[r0:r0 + 64, pair * S + kc * 128: pair * S + kc * 128 + 128],
                        QT[r0:r0 + 64, pair * S + q0 + half * 512: pair * S + q0 + half * 512 + 512],
                        start=True, stop=True))
                p.bar()
                p.op("scalar", lambda e: e.activation(PT[:, 0:512], P1[:, 0:512], AF.Exp, bias=-24.0, scale=0.125))
                p.op("vector", lambda e: e.tensor_scalar(PT[:, 512:1024], P2[:, 0:512], 1.0, None, ALU.mult, ALU.bypass)) if False else None
                p.bar()
                p.op("scalar", lambda e: e.activation(PT[:, 512:1024], P2[:, 0:512], AF.Exp, bias=-24.0, scale=0.125))
                p.bar()
                for half in range(2):
                    p.op("tensor", lambda e, h=h, kc=kc, half=half: e.matmul(
                        psACC[:, half * 512:(half + 1) * 512],
                        VT[:, kc * 256 + h * 64: kc * 256 + h * 64 + 64],
                        PT[:, half * 512:(half + 1) * 512],
                        start=(kc == 0), stop=(kc == 63)))
                    p.op("tensor", lambda e, kc=kc, half=half: e.matmul(
                        psDEN[0:1, half * 512:(half + 1) * 512], c.ones_col[:],
                        PT[:, half * 512:(half + 1) * 512],
                        start=(kc == 0), stop=(kc == 63)))
                p.bar()
            p.op("vector", lambda e: e.reciprocal(recr[:], psDEN[:]))
            p.bar()
            p.op("vector", lambda e: e.tensor_copy(accsb[:], psACC[:]))
            p.bar()
            for half in range(2):
                p.op("tensor", lambda e, half=half: e.matmul(
                    psBC[:, half * 512:(half + 1) * 512], c.ones_row[0:1, 0:64],
                    recr[0:1, half * 512:(half + 1) * 512], start=True, stop=True))
            p.bar()
            p.op("vector", lambda e: e.tensor_mul(abuf[:], accsb[:], psBC[:]))
            p.bar()
            p.op("sync", lambda e, h=h, q0=q0: e.dma_start(
                out=ao_h[h * 64:(h + 1) * 64, q0:q0 + 1024], in_=abuf[:]), dma=True)
            p.bar()
    return c


def host_L6(inputs, x3_2d, r3mu, r3rs):
    S = 8192
    inv_freq = 1.0 / 10000.0 ** (np.arange(0, 64, 2, dtype=np.float64) / 64)
    t = np.arange(S, dtype=np.float64)
    fr = np.outer(t, inv_freq)
    emb = np.concatenate([fr, fr], axis=-1)
    cos = np.cos(emb).T
    sin = np.sin(emb).T
    sins = np.concatenate([-sin[:32], sin[32:]], 0)
    cosT = np.ascontiguousarray(np.tile(cos, (2, 1)).astype(np.float32)).astype(BF)
    sinsT = np.ascontiguousarray(np.tile(sins, (2, 1)).astype(np.float32)).astype(BF)
    maps = []
    for core in range(NCORE):
        b = core // 4
        hg = core % 4
        cols = slice(hg * 256, (hg + 1) * 256)
        x3b = np.ascontiguousarray(x3_2d[b * S:(b + 1) * S].T)
        m = {
            "x3T": x3b,
            "r3mu": np.ascontiguousarray(r3mu[b * S:(b + 1) * S].reshape(1, S)),
            "r3rs": np.ascontiguousarray(r3rs[b * S:(b + 1) * S].reshape(1, S)),
            "qS": np.ascontiguousarray(np.asarray(inputs["L1_ffn_q_w"], np.float32).T[:, cols]),
            "kS": np.ascontiguousarray(np.asarray(inputs["L1_ffn_k_w"], np.float32).T[:, cols]),
            "vS": np.ascontiguousarray(np.asarray(inputs["L1_ffn_v_w"], np.float32).T[:, cols]),
            "qb": np.ascontiguousarray(np.asarray(inputs["L1_ffn_q_b"], np.float32)[cols].reshape(2, 128).T),
            "kb": np.ascontiguousarray(np.asarray(inputs["L1_ffn_k_b"], np.float32)[cols].reshape(2, 128).T),
            "vbrow": np.ascontiguousarray(np.asarray(inputs["L1_ffn_v_b"], np.float32)[cols].reshape(1, 256)),
            "cosT": cosT,
            "sinsT": sinsT,
        }
        maps.append(m)
    return maps

def build_L7():
    c = Ctx()
    T = TPC
    p = c.p
    ao_h = c.inp("aoT", [D, T], BF16)
    x_h = c.inp("x3T", [D, T], F32)
    rmu_h = c.inp("r3mu", [1, T], F32)
    rrs_h = c.inp("r3rs", [1, T], F32)
    gw_h = c.inp("gwT", [D, 8], F32)
    gb_h = c.inp("gbcol", [8, 1], F32)
    ew_h = c.inp("ewT", [8 * D, D], F32)
    eb_h = c.inp("ebrow", [1, 8 * D], F32)
    fw_h = c.inp("ffnT", [D, D], F32)
    fb_h = c.inp("fbcol", [128, 8], F32)
    id_h = c.inp("identf", [128, 128], F32)
    out_h = c.out("outT", [D, T], F32)
    c.setup_common(consts=[0.0, 1.0])
    aoT = c.sb([128, 8 * T], BF16, "aoT_sb")
    for ch in range(8):
        p.op("sync", lambda e, ch=ch: e.dma_start(out=aoT[:, ch * T:(ch + 1) * T], in_=ao_h[ch * 128:(ch + 1) * 128, :]), dma=True)
    gwT = load_w_bf16(c, gw_h, D, 8, "gwT_sb")
    identf = c.sb([128, 128], F32, "identf7")
    identb = c.sb([128, 128], BF16, "identb7")
    p.op("sync", lambda e: e.dma_start(out=identf[:], in_=id_h[:]), dma=True)
    p.op("gpsimd", lambda e: e.dma_start(out=identb[:], in_=id_h[:]), dma=True)
    smalls = {}
    for nm, hh, shp in (("gb", gb_h, [8, 1]), ("fb", fb_h, [128, 8]),
                        ("rmu", rmu_h, [1, T]), ("rrs", rrs_h, [1, T])):
        t = c.sb(shp, F32, nm + "_sb7")
        p.op("sync", lambda e, t=t, hh=hh: e.dma_start(out=t[:], in_=hh[:]), dma=True)
        smalls[nm] = t
    ebrow = c.sb([1, 8 * D], BF16, "ebrow_sb")
    p.op("gpsimd", lambda e: e.dma_start(out=ebrow[:], in_=eb_h[:]), dma=True)
    p.bar()
    P1 = c.ps([128, 512], name="P1_7")
    P2 = c.ps([128, 512], name="P2_7")
    PB = c.ps([128, 256], BF16, name="PB_7")
    glog = c.sb([8, T], F32, "glog")
    for sl in range(0, T, 512):
        for ch in range(8):
            p.op("tensor", lambda e, sl=sl, ch=ch: e.matmul(
                P1[0:8, 0:512], gwT[:, ch * 8:(ch + 1) * 8],
                aoT[:, ch * T + sl: ch * T + sl + 512], start=(ch == 0), stop=(ch == 7)))
        p.bar()
        p.op("vector", lambda e, sl=sl: e.tensor_scalar(
            glog[0:8, sl:sl + 512], P1[0:8, 0:512], smalls["gb"][:, 0:1], None, ALU.add, ALU.bypass))
        p.bar()
    wcols = c.sb([128, 16 * 8], F32, "wcols")
    gt = c.sb([128, 8], F32, "gt7")
    eg = c.sb([128, 8], F32, "eg7")
    r1 = c.sb([128, 1], F32, "r1_7")
    r2 = c.sb([128, 1], F32, "r2_7")
    for tt in range(16):
        p.op("tensor", lambda e, tt=tt: e.transpose(P2[:, 0:8], glog[0:8, tt * 128:(tt + 1) * 128], identf[0:8, 0:8]))
        p.bar()
        p.op("vector", lambda e: e.tensor_copy(gt[:], P2[:, 0:8]))
        p.bar()
        p.op("vector", lambda e: e.reduce_max(r1[:], gt[:], axis=AX.X))
        p.bar()
        p.op("vector", lambda e: e.tensor_scalar(r2[:], r1[:], -1.0, None, ALU.mult, ALU.bypass))
        p.bar()
        p.op("scalar", lambda e: e.activation(eg[:], gt[:], AF.Exp, bias=r2[:, 0:1], scale=1.0))
        p.bar()
        p.op("vector", lambda e: e.reduce_sum(r1[:], eg[:], axis=AX.X))
        p.bar()
        p.op("vector", lambda e: e.reciprocal(r2[:], r1[:]))
        p.bar()
        p.op("vector", lambda e: e.tensor_scalar(gt[:], eg[:], r2[:, 0:1], None, ALU.mult, ALU.bypass))
        p.bar()
        p.op("vector", lambda e: e.reduce_max(r1[:], gt[:], axis=AX.X))
        p.bar()
        p.op("vector", lambda e: e.tensor_single_scalar(eg[:], gt[:], r1[:, 0:1], ALU.is_ge))
        p.bar()
        p.op("vector", lambda e: e.scalar_tensor_tensor(eg[:], eg[:], -1e9, gt[:], ALU.mult, ALU.add))
        p.bar()
        p.op("vector", lambda e: e.reduce_max(r2[:], eg[:], axis=AX.X))
        p.bar()
        p.op("vector", lambda e: e.tensor_single_scalar(eg[:], gt[:], r2[:, 0:1], ALU.is_ge))
        p.bar()
        p.op("vector", lambda e, tt=tt: e.tensor_mul(wcols[:, tt * 8:(tt + 1) * 8], gt[:], eg[:]))
        p.bar()
    ebuf = c.sb([128, 8 * D], BF16, "ebuf")
    macc = c.sb([128, 16 * 1024], F32, "macc")
    for e_ in range(8):
        for ch in range(8):
            p.op("gpsimd", lambda e, e_=e_, ch=ch: e.dma_start(
                out=ebuf[:, ch * D:(ch + 1) * D],
                in_=ew_h[e_ * D + ch * 128: e_ * D + (ch + 1) * 128, :]), dma=True)
        p.bar()
        for tt in range(16):
            for half, ph in ((0, P1), (1, P2)):
                for ch in range(8):
                    p.op("tensor", lambda e, tt=tt, ch=ch, half=half, ph=ph: e.matmul(
                        ph[:, 0:512], aoT[:, ch * T + tt * 128: ch * T + tt * 128 + 128],
                        ebuf[:, ch * D + half * 512: ch * D + half * 512 + 512],
                        start=(ch == 0), stop=False))
                p.op("tensor", lambda e, e_=e_, half=half, ph=ph: e.matmul(
                    ph[:, 0:512], c.ones_row_b[0:1, 0:128],
                    ebrow[0:1, e_ * D + half * 512: e_ * D + (half + 1) * 512], start=False, stop=True))
            p.bar()
            for half, ph in ((0, P1), (1, P2)):
                col = wcols[:, tt * 8 + e_: tt * 8 + e_ + 1]
                dst = macc[:, tt * 1024 + half * 512: tt * 1024 + half * 512 + 512]
                if e_ == 0:
                    p.op("vector", lambda e, ph=ph, col=col, dst=dst: e.tensor_scalar(
                        dst, ph[:, 0:512], col, None, ALU.mult, ALU.bypass))
                else:
                    p.op("vector", lambda e, ph=ph, col=col, dst=dst: e.scalar_tensor_tensor(
                        dst, ph[:, 0:512], col, dst, ALU.mult, ALU.add))
            p.bar()
    moeT = aoT
    stg = ebuf
    for tt in range(16):
        p.op("vector", lambda e, tt=tt: e.tensor_copy(stg[:, 0:1024], macc[:, tt * 1024:(tt + 1) * 1024]))
        p.bar()
        for ch in range(8):
            p.op("tensor", lambda e, ch=ch: e.transpose(PB[:, 0:128], stg[:, ch * 128:(ch + 1) * 128], identb[:]))
            p.bar()
            p.op("vector", lambda e, tt=tt, ch=ch: e.tensor_copy(
                moeT[:, ch * T + tt * 128:(ch * T + tt * 128) + 128], PB[:, 0:128]))
            p.bar()
    ffnw = ebuf
    for ch in range(8):
        p.op("gpsimd", lambda e, ch=ch: e.dma_start(
            out=ffnw[:, ch * D:(ch + 1) * D], in_=fw_h[ch * 128:(ch + 1) * 128, :]), dma=True)
    p.bar()
    mubc = c.sb([128, T], F32, "mubc7")
    rsbc = c.sb([128, T], F32, "rsbc7")
    for sl in range(0, T, 512):
        p.op("tensor", lambda e, sl=sl: e.matmul(P1[:, 0:512], c.ones_row[:], smalls["rmu"][0:1, sl:sl + 512], start=True, stop=True))
        p.bar()
        p.op("vector", lambda e, sl=sl: e.tensor_copy(mubc[:, sl:sl + 512], P1[:, 0:512]))
        p.bar()
        p.op("tensor", lambda e, sl=sl: e.matmul(P1[:, 0:512], c.ones_row[:], smalls["rrs"][0:1, sl:sl + 512], start=True, stop=True))
        p.bar()
        p.op("vector", lambda e, sl=sl: e.tensor_copy(rsbc[:, sl:sl + 512], P1[:, 0:512]))
        p.bar()
    tmp = c.sb([128, 512], F32, "tmp7")
    obuf = c.sb([128, 512], F32, "obuf7")
    xbuf = c.sb([128, 512], F32, "xbuf7")
    for fo in range(8):
        for sl in range(0, T, 512):
            p.op("sync", lambda e, fo=fo, sl=sl: e.dma_start(
                out=xbuf[:], in_=x_h[fo * 128:(fo + 1) * 128, sl:sl + 512]), dma=True)
            for ch in range(8):
                p.op("tensor", lambda e, fo=fo, sl=sl, ch=ch: e.matmul(
                    P2[:, 0:512], ffnw[:, ch * D + fo * 128: ch * D + fo * 128 + 128],
                    moeT[:, ch * T + sl: ch * T + sl + 512], start=(ch == 0), stop=(ch == 7)))
            p.bar()
            p.op("vector", lambda e, sl=sl: e.tensor_sub(tmp[:], xbuf[:], mubc[:, sl:sl + 512]))
            p.bar()
            p.op("vector", lambda e, sl=sl: e.tensor_mul(tmp[:], tmp[:], rsbc[:, sl:sl + 512]))
            p.bar()
            p.op("vector", lambda e, fo=fo: e.scalar_tensor_tensor(
                tmp[:], P2[:, 0:512], smalls["fb"][:, fo:fo + 1], tmp[:], ALU.add, ALU.add))
            p.bar()
            p.op("vector", lambda e: e.tensor_add(obuf[:], tmp[:], xbuf[:]))
            p.bar()
            p.op("sync", lambda e, fo=fo, sl=sl: e.dma_start(
                out=out_h[fo * 128:(fo + 1) * 128, sl:sl + 512], in_=obuf[:]), dma=True)
            p.bar()
    return c


def host_L7(inputs, ao_full_T, x3_2d, r3mu, r3rs):
    identf = np.eye(128, dtype=np.float32)
    gwT = np.ascontiguousarray(np.asarray(inputs["L1_ffn_gate_w"], np.float32).T)
    gbcol = np.ascontiguousarray(np.asarray(inputs["L1_ffn_gate_b"], np.float32).reshape(8, 1))
    ewT = np.ascontiguousarray(np.asarray(inputs["L1_ffn_expert_w"], np.float32).transpose(0, 2, 1).reshape(8 * D, D))
    ebrow = np.ascontiguousarray(np.asarray(inputs["L1_ffn_expert_b"], np.float32))
    ffnT = np.ascontiguousarray(np.asarray(inputs["L1_ffn_ffn_w"], np.float32).T)
    fbcol = np.ascontiguousarray(np.asarray(inputs["L1_ffn_ffn_b"], np.float32).reshape(8, 128).T)
    maps = []
    for i in range(NCORE):
        sl = slice(i * TPC, (i + 1) * TPC)
        maps.append({
            "aoT": np.ascontiguousarray(ao_full_T[:, sl]),
            "x3T": np.ascontiguousarray(x3_2d[sl].T),
            "r3mu": np.ascontiguousarray(r3mu[sl].reshape(1, TPC)),
            "r3rs": np.ascontiguousarray(r3rs[sl].reshape(1, TPC)),
            "gwT": gwT, "gbcol": gbcol, "ewT": ewT, "ebrow": ebrow.reshape(1, 8 * D),
            "ffnT": ffnT, "fbcol": fbcol, "identf": identf,
        })
    return maps


def kernel(**inputs):
    x = np.asarray(inputs["L0_attn_x"], np.float32).reshape(TOK, D)
    c = build_L1()
    res = c.run(host_L1(inputs), core_ids=list(range(NCORE)))
    hT_full = np.concatenate([np.asarray(r["hT"]) for r in res.results], axis=1)
    c = build_L2(NM=512)
    res = c.run(host_L2(inputs, hT_full, NM=512), core_ids=[0, 1, 2, 3])
    outs_streams = {}
    for i, (b, h) in enumerate([(0, 0), (0, 1), (1, 0), (1, 1)]):
        outs_streams[(b, h)] = np.asarray(res.results[i]["outs"])
    c = build_L3()
    res = c.run(host_L3(inputs, outs_streams, x), core_ids=list(range(NCORE)))
    x1 = np.concatenate([np.asarray(r["x1T"], np.float32).T for r in res.results], 0)
    c = build_L4()
    res = c.run(host_L4(inputs, x1), core_ids=list(range(NCORE)))
    x2 = np.concatenate([np.asarray(r["x2T"], np.float32).T for r in res.results], 0)
    r2rows = [np.asarray(r["r2row"], np.float32) for r in res.results]
    parts = [np.asarray(r["part"], np.float32) for r in res.results]
    c = build_L5()
    res = c.run(host_L5(inputs, x2, r2rows, parts), core_ids=list(range(NCORE)))
    x3 = np.concatenate([np.asarray(r["x3T"], np.float32).T for r in res.results], 0)
    r3mu = np.concatenate([np.asarray(r["r3row"], np.float32)[0] for r in res.results])
    r3rs = np.concatenate([np.asarray(r["r3row"], np.float32)[1] for r in res.results])
    c, handles = build_L6()
    build_L6_attn(c, handles)
    res = c.run(host_L6(inputs, x3, r3mu, r3rs), core_ids=list(range(NCORE)))
    ao_full_T = np.zeros((D, TOK), dtype=BF)
    for core in range(NCORE):
        b, hg = core // 4, core % 4
        ao_full_T[hg * 256:(hg + 1) * 256, b * 8192:(b + 1) * 8192] = np.asarray(res.results[core]["aoT"])
    c = build_L7()
    res = c.run(host_L7(inputs, ao_full_T, x3, r3mu, r3rs), core_ids=list(range(NCORE)))
    out = np.concatenate([np.asarray(r["outT"], np.float32).T for r in res.results], 0)
    return np.ascontiguousarray(out.reshape(2, 8192, D))
